# version 09
# baseline (speedup 1.0000x reference)
import numpy as np
import ml_dtypes
from contextlib import ExitStack

import concourse.bass as bass
import concourse.mybir as mybir
from concourse.bass_utils import run_bass_kernel_spmd

F32 = mybir.dt.float32
BF16 = mybir.dt.bfloat16
AF = mybir.ActivationFunctionType
ALU = mybir.AluOpType
AX = mybir.AxisListType.X
BF = ml_dtypes.bfloat16

NCORES = 8
D = 4096
T = 2048
TH = 1024
NCD = 32
L0_HSEL = [0, 4, 8, 12, 16, 20, 24, 28]
L0_EPS = 1e-6
LN_EPS = 1e-5
H1, DH, BLK = 32, 128, 256
LAYER_IDX, N_LAYERS = 1, 32
RMS_EPS = 1e-5
BIG = 1.0e30

DEBUG_OUT = ()


def _slopes():
    base = 1.0 / 2.0 ** (8.0 / H1)
    expo = np.arange(H1, dtype=np.float64) + 1.0
    factor = 1.0 - LAYER_IDX / (N_LAYERS - 1 + 1e-05) + 1e-05
    return base ** expo * factor


BD_TAB = [float(v) for v in np.exp(-_slopes() * BLK)]


def _f32(a):
    return np.ascontiguousarray(np.asarray(a), dtype=np.float32)


def _bf(a):
    return np.ascontiguousarray(np.asarray(a, dtype=np.float32).astype(BF))


def _slabs(WT):
    d_in, d_out = WT.shape
    return np.ascontiguousarray(WT.reshape(d_in, d_out // 128, 128).transpose(1, 0, 2))


def host_prep(inputs):
    h = {}
    x = np.asarray(inputs["L0_attn_hidden_states"]).reshape(NCORES * T, D)
    h["x_tm"] = [_f32(x[c * T:(c + 1) * T]) for c in range(NCORES)]
    h["x_fm"] = [_f32(x[c * T:(c + 1) * T].T) for c in range(NCORES)]

    Wq = np.asarray(inputs["L0_attn_Wq"])
    bq = np.asarray(inputs["L0_attn_bq"])
    rows = np.concatenate([np.arange(hh * 128, (hh + 1) * 128) for hh in L0_HSEL])
    h["wq_slabs"] = _f32(_slabs(Wq[rows].T))
    h["bq_sel"] = _f32(bq[rows].reshape(8, 128).T)
    h["Mt"] = _f32(inputs["L0_attn_M"])
    h["z8"] = _f32(np.asarray(inputs["L0_attn_z"]).T)

    Wo = np.asarray(inputs["L0_attn_Wo"])
    Wb = np.asarray(inputs["L0_ffn_W_base"])
    B0 = np.asarray(inputs["L0_ffn_B"])
    B_allM = B0.transpose(0, 2, 1).reshape(128, 4096)
    h["l0out_slabs"] = _f32(np.concatenate(
        [_slabs(Wo.T), _slabs(Wb.T), _slabs(B_allM)], axis=1))
    h["b_base"] = _f32(np.asarray(inputs["L0_ffn_b_base"]).reshape(32, 128).T)
    h["a_allT"] = _f32(np.asarray(inputs["L0_ffn_A"]).reshape(128, D).T)
    h["wgT"] = _f32(np.asarray(inputs["L0_ffn_W_gate"]).T)

    qkvT = np.asarray(inputs["L1_attn_qkv_w"]).T
    h["qkv_slabs"] = _f32(qkvT.reshape(D, 32, 3, 128).transpose(1, 2, 0, 3))
    h["gw1_slabs"] = _f32(np.concatenate(
        [_slabs(np.asarray(inputs["L1_attn_gate_w"]).T),
         _slabs(np.asarray(inputs["L1_ffn_W"]).T)], axis=1))
    h["outw_slabs"] = _f32(_slabs(np.asarray(inputs["L1_attn_out_w"]).T))
    h["b_l1"] = _f32(np.asarray(inputs["L1_ffn_b"]).reshape(32, 128).T)
    h["norm_w"] = _f32(np.asarray(inputs["L1_attn_norm_w"]).reshape(32, 128).T)
    h["a_l1"] = _f32(inputs["L1_ffn_A"])
    h["limesT"] = _f32(np.asarray(inputs["L1_ffn_LiMEs"]).T)
    h["limes"] = _f32(inputs["L1_ffn_LiMEs"])
    h["bmat"] = _f32(inputs["L1_ffn_Bmat"])

    sl = _slopes()
    r = np.arange(BLK, dtype=np.float64)
    j, i = r[:, None], r[None, :]
    diag = np.where(i - j >= 0, np.exp(-sl[:, None, None] * (i - j)), 0.0)
    h["diagT"] = _f32(diag.reshape(H1, 2, 128, BLK))
    qd = np.exp(-sl[:, None] * (r[None, :] + 1.0))
    h["qd_tab"] = _bf(np.broadcast_to(qd[:, None, :], (H1, 128, BLK)))
    bd = np.exp(-sl * BLK)
    ii = np.arange(T, dtype=np.float64)
    qdbd = np.exp(-sl[:, None] * ((ii % BLK) + 1.0)[None, :]) * \
        (bd[:, None] ** (ii // BLK)[None, :])
    h["qdbd_tab"] = _bf(np.broadcast_to(qdbd[:, None, :], (H1, 128, T)))
    kd = np.exp(-sl[:, None] * (255.0 - r)[None, :])
    h["kdec"] = _f32(kd.reshape(H1, 2, 128).transpose(2, 0, 1).reshape(128, 64))

    h["ones_bf"] = _bf(np.ones((128, 1)))
    h["ones_f32r"] = _f32(np.ones((1, 128)))
    h["ident_bf"] = _bf(np.eye(128))
    h["ident_f"] = _f32(np.eye(128))
    e16 = np.zeros((8, 128), dtype=np.float32)
    for e in range(8):
        e16[e, e * 16:(e + 1) * 16] = 1.0
    h["e16"] = _bf(e16)
    s4 = np.zeros((128, 16), dtype=np.float32)
    for j in range(4):
        s4[:, j * 4 + j] = 1.0
    h["sel4"] = _bf(s4)
    cst = np.zeros((128, 4), dtype=np.float32)
    cst[:, 0] = LN_EPS
    cst[:, 1] = RMS_EPS
    cst[:, 2] = BIG
    cst[:, 3] = 1.0
    h["cst"] = cst
    coefb = []
    for c in range(NCORES):
        k = c % 4
        cb = np.zeros((4, 128, H1), dtype=np.float32)
        for s in range(4):
            if s < k:
                for hh in range(H1):
                    cb[s, :, hh] = bd[hh] ** (8 * (k - 1 - s))
        coefb.append(cb)
    h["coefb"] = coefb
    return h


def build_fused():
    nc = bass.Bass("TRN2", target_bir_lowering=False)

    def din(name, shape, dt=F32):
        return nc.dram_tensor(name, shape, dt, kind="ExternalInput").ap()

    def dout(name, shape, dt=F32):
        return nc.dram_tensor(name, shape, dt, kind="ExternalOutput").ap()

    def dint(name, shape, dt=F32):
        kind = "ExternalOutput" if name in DEBUG_OUT else "Internal"
        return nc.dram_tensor(name, shape, dt, kind=kind).ap()

    x_tm = din("x_tm", [T, D])
    x_fm = din("x_fm", [D, T])
    wq_slabs = din("wq_slabs", [8, D, 128])
    bq_sel = din("bq_sel", [128, 8])
    Mt = din("Mt", [8, 128, 128])
    z8 = din("z8", [128, 8])
    l0out_slabs = din("l0out_slabs", [32, 8320, 128])
    b_base = din("b_base", [128, 32])
    a_allT = din("a_allT", [D, 128])
    wgT = din("wgT", [D, 8])
    qkv_slabs = din("qkv_slabs", [32, 3, D, 128])
    gw1_slabs = din("gw1_slabs", [32, 2 * D, 128])
    b_l1 = din("b_l1", [128, 32])
    a_l1 = din("a_l1", [D, 32])
    limesT = din("limesT", [32, 8])
    diagT = din("diagT", [H1, 2, 128, BLK])
    qd_tab = din("qd_tab", [H1, 128, BLK], BF16)
    qdbd_tab = din("qdbd_tab", [H1, 128, T], BF16)
    kdec = din("kdec", [128, 64])
    ones_bf = din("ones_bf", [128, 1], BF16)
    ones_f32r = din("ones_f32r", [1, 128])
    ident_bf = din("ident_bf", [128, 128], BF16)
    ident_f = din("ident_f", [128, 128])
    e16 = din("e16", [8, 128], BF16)
    cst = din("cst", [128, 4])
    limes = din("limes", [8, 32])
    bmat = din("bmat", [32, 4096])
    norm_w = din("norm_w", [128, 32])
    outw_slabs = din("outw_slabs", [32, D, 128])
    coefb4 = din("coefb4", [4, 128, H1])
    sel4 = din("sel4", [128, 16], BF16)

    xn_dram = dint("xn_dram", [T, D], BF16)
    x1_dram = dint("x1_dram", [NCD, 128, T])
    mxb = dint("mxb", [16, 1])

    r_out = dint("r_out", [NCD, 128, T])
    gate_out = dint("gate_out", [NCD, 128, T], BF16)
    attnp_out = dint("attnp_out", [H1, 128, T])
    qdbd_out = dint("qdbd_out", [H1, 128, T], BF16)
    s_src = dint("s_src", [H1 * 128, 128])
    sall1 = dint("sall1", [4 * 16 * 128, 128])
    sall2 = dint("sall2", [4 * 16 * 128, 128])
    mx_src = dint("mx_src", [2, 1])
    mxg = dint("mxg", [16, 1])
    corr_dram = dint("corr_dram", [H1, 128, T])
    rms_dram = dint("rms_dram", [4, 512])
    st_dram = dint("st_dram", [2, 2, 1024])
    out2 = dout("out2", [D, T])

    actT = nc.alloc_sbuf_tensor("actT", [128, NCD * TH], BF16)
    scr = nc.alloc_sbuf_tensor("scr", [128, 9 * TH], BF16)
    wslab = nc.alloc_sbuf_tensor("wslab", [128, 2 * 8320], BF16)
    f32buf = nc.alloc_sbuf_tensor("f32buf", [128, 4096], F32)
    bcast = nc.alloc_sbuf_tensor("bcast", [128, 2 * TH], F32)
    attnp_sb = nc.alloc_sbuf_tensor("attnp_sb", [128, TH], F32)
    diag_sb = nc.alloc_sbuf_tensor("diag_sb", [128, 1024], F32)
    qd_sb = nc.alloc_sbuf_tensor("qd_sb", [128, 2 * BLK], BF16)
    qdbd_sb = nc.alloc_sbuf_tensor("qdbd_sb", [128, TH], BF16)
    kv_sb = nc.alloc_sbuf_tensor("kv_sb", [128, 256], F32)
    kvb_sb = nc.alloc_sbuf_tensor("kvb_sb", [128, 256], BF16)
    atw = nc.alloc_sbuf_tensor("atw", [128, 1536], BF16)
    ltile = nc.alloc_sbuf_tensor("ltile", [128, 512], F32)
    rowbuf = nc.alloc_sbuf_tensor("rowbuf", [1, 3 * T], F32)
    colbuf = nc.alloc_sbuf_tensor("colbuf", [128, 96], F32)
    bias_sb = nc.alloc_sbuf_tensor("bias_sb", [128, 112], F32)
    mz_sb = nc.alloc_sbuf_tensor("mz_sb", [128, 8 * 128 + 8], BF16)
    smal = nc.alloc_sbuf_tensor("smal", [128, 512], BF16)
    onesr = nc.alloc_sbuf_tensor("onesr", [1, 128], F32)
    idf = nc.alloc_sbuf_tensor("idf", [128, 128], F32)
    lgT = nc.alloc_sbuf_tensor("lgT", [8, T], BF16)
    zt_sb = nc.alloc_sbuf_tensor("zt_sb", [32, T], F32)
    hs_sb = nc.alloc_sbuf_tensor("hs_sb", [8, T], F32)
    mrow = nc.alloc_sbuf_tensor("mrow", [1, 32], F32)
    limes_sb = nc.alloc_sbuf_tensor("limes_sb", [8, 32], BF16)
    nw_sb = nc.alloc_sbuf_tensor("nw_sb", [128, 32], F32)
    coefq = nc.alloc_sbuf_tensor("coefq", [128, 4 * H1], F32)

    A = [nc.alloc_psum_tensor(f"psA{i}", [128, 512], F32) for i in range(4)]
    Bp = [nc.alloc_psum_tensor(f"psB{i}", [128, 512], F32) for i in range(2)]
    Cp = nc.alloc_psum_tensor("psC", [128, 1024], BF16)
    Dp = nc.alloc_psum_tensor("psD", [128, 512], F32)

    with ExitStack() as es:
        sn = ["wl", "al", "xb", "mm", "ac", "ve", "gs", "xs", "vf", "af",
              "pf", "c1", "c2", "c3"]
        SM = {n: es.enter_context(nc.semaphore(n)) for n in sn}
        C = {n: 0 for n in sn}
        block = es.enter_context(nc.Block())

        def inc(n, v=16):
            C[n] += v
            return SM[n], v

        def wall(eng, *names):
            for n in names:
                if C[n] > 0:
                    eng.wait_ge(SM[n], C[n])

        def vF(v, ins):
            C["vf"] += 1
            ins.then_inc(SM["vf"], 1)
            v.wait_ge(SM["vf"], C["vf"])

        def aF(sc, ins):
            C["af"] += 1
            ins.then_inc(SM["af"], 1)
            sc.wait_ge(SM["af"], C["af"])

        @block.sync
        def _(s):
            s.dma_start(out=bias_sb[:, 0:8], in_=bq_sel).then_inc(*inc("al"))
            s.dma_start(out=bias_sb[:, 8:40], in_=b_base).then_inc(*inc("al"))
            s.dma_start(out=bias_sb[:, 40:72], in_=b_l1).then_inc(*inc("al"))
            s.dma_start(out=colbuf[:, 0:64], in_=kdec).then_inc(*inc("al"))
            s.dma_start(out=onesr[:, :], in_=ones_f32r).then_inc(*inc("al"))
            s.dma_start(out=idf[:, :], in_=ident_f).then_inc(*inc("al"))
            s.dma_start(out=bias_sb[:, 80:84], in_=cst).then_inc(*inc("al"))
            s.dma_start(out=nw_sb[:, :], in_=norm_w).then_inc(*inc("al"))
            s.dma_start(out=coefq[:, :].rearrange("p (s h) -> p s h", s=4),
                        in_=coefb4.rearrange("s p h -> p s h"))\
                .then_inc(*inc("al"))

        @block.gpsimd
        def _(g):
            g.dma_start(out=mz_sb[:, 0:1024].rearrange("p (h e) -> p h e", h=8),
                        in_=Mt.rearrange("h d e -> d h e")).then_inc(*inc("wl"))
            g.dma_start(out=mz_sb[:, 1024:1032], in_=z8).then_inc(*inc("wl"))
            g.dma_start(out=smal[:, 0:128], in_=ident_bf).then_inc(*inc("wl"))
            g.dma_start(out=smal[:, 128:129], in_=ones_bf).then_inc(*inc("wl"))
            g.dma_start(out=smal[0:8, 129:257], in_=e16).then_inc(*inc("wl"))
            g.dma_start(out=smal[0:32, 257:265], in_=limesT).then_inc(*inc("wl"))
            g.dma_start(out=limes_sb[0:8, :], in_=limes).then_inc(*inc("wl"))
            g.dma_start(out=smal[:, 272:288], in_=sel4).then_inc(*inc("wl"))

        @block.vector
        def _(v):
            wall(v, "al")
            v.tensor_scalar_mul(bias_sb[:, 72:80], bias_sb[:, 0:8], -1.0)\
                .then_inc(*inc("ve", 1))

        al0, ac0, ve0, xs0 = C["al"], C["ac"], C["ve"], C["xs"]
        xa = f32buf[:, 0:4096]
        sqb = bcast[:, 0:2048]

        @block.sync
        def _(s):
            for i in range(16):
                if i >= 1:
                    s.wait_ge(SM["ac"], ac0 + 2 * i)
                    s.wait_ge(SM["ve"], ve0 + 2 * i - 1)
                s.dma_start(out=xa, in_=x_tm[i * 128:(i + 1) * 128, :])\
                    .then_inc(*inc("al"))
                s.wait_ge(SM["ac"], ac0 + 2 * i + 2)
                s.dma_start(out=xn_dram[i * 128:(i + 1) * 128, :],
                            in_=scr[:, 4096:8192]).then_inc(*inc("xs"))

        @block.vector
        def _(v):
            for i in range(16):
                v.wait_ge(SM["al"], al0 + 16 * (i + 1))
                v.reduce_sum(colbuf[:, 64:65], xa, AX)
                v.tensor_mul(sqb, xa[:, 0:2048], xa[:, 0:2048])
                v.reduce_sum(colbuf[:, 66:67], sqb, AX)
                v.tensor_mul(sqb, xa[:, 2048:4096], xa[:, 2048:4096])
                vF(v, v.reduce_sum(colbuf[:, 67:68], sqb, AX))
                vF(v, v.tensor_add(colbuf[:, 65:66], colbuf[:, 66:67],
                                   colbuf[:, 67:68]))
                v.tensor_scalar_mul(colbuf[:, 68:69], colbuf[:, 64:65], 1.0 / D)
                vF(v, v.tensor_scalar_mul(colbuf[:, 69:70], colbuf[:, 65:66],
                                          1.0 / D))
                vF(v, v.tensor_mul(colbuf[:, 70:71], colbuf[:, 68:69],
                                   colbuf[:, 68:69]))
                v.tensor_sub(colbuf[:, 71:72], colbuf[:, 69:70], colbuf[:, 70:71])\
                    .then_inc(*inc("ve", 1))
                v.wait_ge(SM["ac"], ac0 + 2 * i + 1)
                vF(v, v.reciprocal(colbuf[:, 73:74], colbuf[:, 72:73]))
                vF(v, v.tensor_mul(colbuf[:, 74:75], colbuf[:, 68:69],
                                   colbuf[:, 73:74]))
                v.tensor_scalar_mul(colbuf[:, 75:76], colbuf[:, 74:75], -1.0)\
                    .then_inc(*inc("ve", 1))

        @block.scalar
        def _(sc):
            for i in range(16):
                sc.wait_ge(SM["ve"], ve0 + 2 * i + 1)
                sc.activation(colbuf[:, 72:73], colbuf[:, 71:72], AF.Sqrt,
                              bias=bias_sb[:, 80:81]).then_inc(*inc("ac", 1))
                sc.wait_ge(SM["ve"], ve0 + 2 * i + 2)
                if i >= 1:
                    sc.wait_ge(SM["xs"], xs0 + 16 * i)
                sc.activation(scr[:, 4096:8192], xa, AF.Identity,
                              bias=colbuf[:, 75:76], scale=colbuf[:, 73:74])\
                    .then_inc(*inc("ac", 1))

        def load_xnT():
            @block.sync
            def _(s):
                wall(s, "xs", "ve", "mm")

            @block.sync
            def _(s):
                pass
            return None

        def chain(nd, lhs_tile, rhs_tile, evac, banks=(0, 1), wait_before=None,
                  wait_at=None):
            stops = []

            @block.tensor
            def _(t):
                if wait_before:
                    wait_before(t)
                for d in range(nd):
                    if wait_at and d in wait_at:
                        t.wait_ge(SM["ve"], wait_at[d])
                    for tc in range(2):
                        ins = t.matmul(A[banks[tc]][:, :], lhs_tile(d),
                                       rhs_tile(d, tc), start=(d == 0),
                                       stop=(d == nd - 1))
                        if d == nd - 1:
                            ins.then_inc(*inc("mm", 1))
                            stops.append(C["mm"])
            for tc in range(2):
                evac(tc, A[banks[tc]], stops[tc])
            return stops

        def xnT_fill(hf):
            @block.sync
            def _(s):
                if hf == 0:
                    s.wait_ge(SM["xs"], xs0 + 128)
                else:
                    wall(s, "xs", "ve", "mm")
                for c in range(NCD):
                    s.dma_start(out=actT[:, c * TH:(c + 1) * TH],
                                in_=xn_dram[hf * TH:(hf + 1) * TH,
                                            c * 128:(c + 1) * 128],
                                transpose=True).then_inc(*inc("xb"))

        def run_half_l0(hf):
            xnT_fill(hf)
            xb_need = C["xb"]

            for h in range(8):
                wbase = (h % 2) * 8320

                @block.gpsimd
                def _(g, h=h, wbase=wbase):
                    g.wait_ge(SM["mm"], C["mm"])
                    g.dma_start(
                        out=wslab[:, wbase:wbase + 4096]
                        .rearrange("p (c m) -> p c m", c=32),
                        in_=wq_slabs[h].rearrange("(c p) m -> p c m", p=128))\
                        .then_inc(*inc("wl"))
                wl_need = C["wl"]
                sig = scr[:, 8 * TH:9 * TH]

                def lhs(d, wbase=wbase):
                    return wslab[:, wbase + d * 128:wbase + (d + 1) * 128]

                def rhs(d, tc):
                    return actT[:, d * TH + tc * 512:d * TH + (tc + 1) * 512]

                def evac_sig(tc, ps, mmv, h=h, sig=sig):
                    @block.scalar
                    def _(sc, tc=tc, mmv=mmv, h=h):
                        sc.wait_ge(SM["mm"], mmv)
                        sc.wait_ge(SM["ve"], C["ve"])
                        sc.activation(ltile[:, 0:512], ps[:, :], AF.Relu,
                                      bias=bias_sb[:, h:h + 1])
                        sc.activation(diag_sb[:, 0:512], ps[:, :], AF.Relu,
                                      bias=bias_sb[:, 72 + h:73 + h], scale=-1.0)\
                            .then_inc(*inc("ac", 1))
                        sc.activation(attnp_sb[:, 0:512], diag_sb[:, 0:512],
                                      AF.Exp, scale=-1.0).then_inc(*inc("ac", 1))
                    acv = C["ac"]

                    @block.vector
                    def _(v, tc=tc, acv=acv, sig=sig):
                        v.wait_ge(SM["ac"], acv)
                        v.tensor_add(sig[:, tc * 512:(tc + 1) * 512],
                                     ltile[:, 0:512], attnp_sb[:, 0:512])\
                            .then_inc(*inc("ve", 1))

                def wait_b(t, wl_need=wl_need):
                    t.wait_ge(SM["wl"], wl_need)
                    t.wait_ge(SM["xb"], xb_need)
                    if C["ac"] > 0:
                        t.wait_ge(SM["ac"], C["ac"])
                    if C["ve"] > 0:
                        t.wait_ge(SM["ve"], C["ve"])

                chain(32, lhs, rhs, evac_sig, banks=(0, 1), wait_before=wait_b)

                for tc in range(2):
                    ve_sig = C["ve"]

                    @block.tensor
                    def _(t, h=h, tc=tc, ve_sig=ve_sig, sig=sig):
                        t.wait_ge(SM["ve"], ve_sig)
                        t.matmul(Bp[tc][:, :], mz_sb[:, h * 128:(h + 1) * 128],
                                 sig[:, tc * 512:(tc + 1) * 512],
                                 start=True, stop=True).then_inc(*inc("mm", 1))
                        t.matmul(Dp[0:1, :], mz_sb[:, 1024 + h:1025 + h],
                                 sig[:, tc * 512:(tc + 1) * 512],
                                 start=True, stop=True).then_inc(*inc("mm", 1))
                    mm_nd = C["mm"]

                    @block.vector
                    def _(v, mm_nd=mm_nd):
                        v.wait_ge(SM["mm"], mm_nd)
                        vF(v, v.tensor_copy(rowbuf[0:1, 2 * T:2 * T + 512],
                                            Dp[0:1, :]))
                        vF(v, v.tensor_scalar_add(
                            rowbuf[0:1, 2 * T + 512:2 * T + 1024],
                            rowbuf[0:1, 2 * T:2 * T + 512], L0_EPS))
                        v.reciprocal(rowbuf[0:1, 2 * T:2 * T + 512],
                                     rowbuf[0:1, 2 * T + 512:2 * T + 1024])\
                            .then_inc(*inc("ve", 1))
                    ve_den = C["ve"]

                    @block.tensor
                    def _(t, ve_den=ve_den):
                        t.wait_ge(SM["ve"], ve_den)
                        t.matmul(A[2][:, :], onesr[0:1, :],
                                 rowbuf[0:1, 2 * T:2 * T + 512],
                                 start=True, stop=True).then_inc(*inc("mm", 1))
                    mm_bc = C["mm"]

                    @block.vector
                    def _(v, h=h, tc=tc, mm_bc=mm_bc):
                        v.wait_ge(SM["mm"], mm_bc)
                        if h == 4 and tc == 0:
                            v.wait_ge(SM["xs"], xs0 + 256)
                        v.tensor_copy(attnp_sb[:, 512:1024], Bp[tc][:, :])
                        v.tensor_mul(
                            scr[:, h * TH + tc * 512:h * TH + (tc + 1) * 512],
                            attnp_sb[:, 512:1024], A[2][:, :])\
                            .then_inc(*inc("ve", 1))

            ve_preS3 = C["ve"]
            ac_preS3 = C["ac"]

            @block.gpsimd
            def _(g):
                g.wait_ge(SM["mm"], C["mm"])
                g.dma_start(out=wslab[:, 0:4096]
                            .rearrange("p (c m) -> p c m", c=32),
                            in_=a_allT.rearrange("(c p) m -> p c m", p=128))\
                    .then_inc(*inc("wl"))
                g.dma_start(out=wslab[:, 4096:4352]
                            .rearrange("p (c m) -> p c m", c=32),
                            in_=wgT.rearrange("(c p) m -> p c m", p=128))\
                    .then_inc(*inc("wl"))
            wl_s3 = C["wl"]
            gbuf = scr[:, 8 * TH:9 * TH]

            def lhs_g(d):
                return wslab[:, d * 128:(d + 1) * 128]

            def rhs_x(d, tc):
                return actT[:, d * TH + tc * 512:d * TH + (tc + 1) * 512]

            def evac_g(tc, ps, mmv):
                @block.vector
                def _(v, tc=tc, mmv=mmv):
                    v.wait_ge(SM["mm"], mmv)
                    v.tensor_copy(gbuf[:, tc * 512:(tc + 1) * 512], ps[:, :])\
                        .then_inc(*inc("ve", 1))

            def wait_g(t):
                t.wait_ge(SM["wl"], wl_s3)
                t.wait_ge(SM["ve"], C["ve"])

            chain(32, lhs_g, rhs_x, evac_g, banks=(0, 1), wait_before=wait_g)
            ve_s3g = C["ve"]

            for tc in range(2):
                @block.tensor
                def _(t, tc=tc):
                    t.wait_ge(SM["ve"], C["ve"])
                    for d in range(32):
                        ins = t.matmul(Bp[0][0:8, :],
                                       wslab[:, 4096 + d * 8:4096 + (d + 1) * 8],
                                       rhs_x(d, tc), start=(d == 0),
                                       stop=(d == 31))
                    ins.then_inc(*inc("mm", 1))
                mm_l = C["mm"]

                @block.vector
                def _(v, tc=tc, mm_l=mm_l):
                    v.wait_ge(SM["mm"], mm_l)
                    v.tensor_copy(
                        lgT[0:8, hf * TH + tc * 512:hf * TH + (tc + 1) * 512],
                        Bp[0][0:8, :]).then_inc(*inc("ve", 1))

            rwT = atw[0:8, 256:1280]
            for it in range(8):
                col = hf * TH + it * 128
                ve_l = C["ve"]

                @block.tensor
                def _(t, col=col, ve_l=ve_l):
                    t.wait_ge(SM["ve"], ve_l)
                    t.transpose(Cp[:, 0:8], lgT[0:8, col:col + 128],
                                smal[0:8, 0:8]).then_inc(*inc("mm", 1))
                mm_t = C["mm"]

                @block.vector
                def _(v, mm_t=mm_t):
                    v.wait_ge(SM["mm"], mm_t)
                    vF(v, v.tensor_copy(ltile[:, 0:8], Cp[:, 0:8]))
                    vF(v, v.reduce_max(colbuf[:, 76:77], ltile[:, 0:8], AX))
                    v.tensor_scalar_mul(colbuf[:, 77:78], colbuf[:, 76:77], -1.0)\
                        .then_inc(*inc("ve", 1))
                ve_m = C["ve"]

                @block.scalar
                def _(sc, ve_m=ve_m):
                    sc.wait_ge(SM["ve"], ve_m)
                    sc.activation(ltile[:, 8:16], ltile[:, 0:8], AF.Exp,
                                  bias=colbuf[:, 77:78]).then_inc(*inc("ac", 1))
                ac_e = C["ac"]

                @block.vector
                def _(v, ac_e=ac_e):
                    v.wait_ge(SM["ac"], ac_e)
                    vF(v, v.reduce_sum(colbuf[:, 78:79], ltile[:, 8:16], AX))
                    vF(v, v.reciprocal(colbuf[:, 79:80], colbuf[:, 78:79]))
                    vF(v, v.tensor_single_scalar(ltile[:, 16:24], ltile[:, 8:16],
                                                 colbuf[:, 79:80], ALU.mult))
                    vF(v, v.reduce_max(colbuf[:, 80:81], ltile[:, 16:24], AX))
                    v.tensor_single_scalar(ltile[:, 24:32], ltile[:, 16:24],
                                           colbuf[:, 80:81], ALU.subtract)\
                        .then_inc(*inc("ve", 1))
                ve_d = C["ve"]

                @block.scalar
                def _(sc, ve_d=ve_d):
                    sc.wait_ge(SM["ve"], ve_d)
                    aF(sc, sc.activation(ltile[:, 32:40], ltile[:, 24:32],
                                         AF.Sign))
                    sc.activation(ltile[:, 40:48], ltile[:, 32:40], AF.Identity,
                                  bias=bias_sb[:, 82:83], scale=bias_sb[:, 82:83])\
                        .then_inc(*inc("ac", 1))
                ac_s = C["ac"]

                @block.vector
                def _(v, it=it, ac_s=ac_s):
                    v.wait_ge(SM["ac"], ac_s)
                    vF(v, v.tensor_sub(ltile[:, 48:56], ltile[:, 16:24],
                                       ltile[:, 40:48]))
                    vF(v, v.reduce_max(colbuf[:, 81:82], ltile[:, 48:56], AX))
                    vF(v, v.tensor_single_scalar(ltile[:, 56:64], ltile[:, 16:24],
                                                 colbuf[:, 81:82], ALU.is_ge))
                    vF(v, v.tensor_mul(ltile[:, 64:72], ltile[:, 16:24],
                                       ltile[:, 56:64]))
                    vF(v, v.reduce_sum(colbuf[:, 82:83], ltile[:, 64:72], AX))
                    vF(v, v.reciprocal(colbuf[:, 83:84], colbuf[:, 82:83]))
                    vF(v, v.tensor_single_scalar(ltile[:, 72:80], ltile[:, 64:72],
                                                 colbuf[:, 83:84], ALU.mult))
                    v.tensor_copy(atw[:, 0:8], ltile[:, 72:80])\
                        .then_inc(*inc("ve", 1))
                ve_r = C["ve"]

                @block.tensor
                def _(t, ve_r=ve_r):
                    t.wait_ge(SM["ve"], ve_r)
                    t.transpose(Cp[0:8, 0:128], atw[:, 0:8], smal[:, 0:128])\
                        .then_inc(*inc("mm", 1))
                mm_rt = C["mm"]

                @block.vector
                def _(v, it=it, mm_rt=mm_rt):
                    v.wait_ge(SM["mm"], mm_rt)
                    v.tensor_copy(rwT[0:8, it * 128:(it + 1) * 128],
                                  Cp[0:8, 0:128]).then_inc(*inc("ve", 1))

            for tc in range(2):
                ve_rw = C["ve"]

                @block.tensor
                def _(t, tc=tc, ve_rw=ve_rw):
                    t.wait_ge(SM["ve"], ve_rw)
                    t.matmul(Bp[1][:, :], smal[0:8, 129:257],
                             rwT[0:8, tc * 512:(tc + 1) * 512],
                             start=True, stop=True).then_inc(*inc("mm", 1))
                mm_e = C["mm"]

                @block.vector
                def _(v, tc=tc, mm_e=mm_e):
                    v.wait_ge(SM["mm"], mm_e)
                    v.tensor_mul(
                        gbuf[:, tc * 512:(tc + 1) * 512],
                        gbuf[:, tc * 512:(tc + 1) * 512], Bp[1][:, :])\
                        .then_inc(*inc("ve", 1))
            hw = gbuf
            ve_hw = C["ve"]

            slabF = [None, None]
            bankF = [None, None]
            xstrF = [None, None]
            outF = [None, None]
            for o in range(32):
                p = o % 2
                wbase = p * 8320

                @block.gpsimd
                def _(g, o=o, p=p, wbase=wbase):
                    if slabF[p] is None:
                        g.wait_ge(SM["mm"], C["mm"])
                    else:
                        g.wait_ge(SM["mm"], slabF[p])
                    g.dma_start(
                        out=wslab[:, wbase:wbase + 8320]
                        .rearrange("p (c m) -> p c m", c=65),
                        in_=l0out_slabs[o].rearrange("(c p) m -> p c m", p=128))\
                        .then_inc(*inc("wl"))
                wl_o = C["wl"]

                @block.gpsimd
                def _(g, wbase=wbase, wl_o=wl_o):
                    g.wait_ge(SM["wl"], wl_o)
                    wo4 = wslab[:, wbase:wbase + 4096]\
                        .rearrange("p (h g m) -> p h g m", h=8, g=4)
                    g.tensor_add(wo4[:, :, 0, :], wo4[:, :, 0, :],
                                 wo4[:, :, 1, :]).then_inc(*inc("pf", 1))
                    g.tensor_add(wo4[:, :, 2, :], wo4[:, :, 2, :],
                                 wo4[:, :, 3, :]).then_inc(*inc("pf", 1))
                    g.wait_ge(SM["pf"], C["pf"])
                    g.tensor_add(wo4[:, :, 0, :], wo4[:, :, 0, :],
                                 wo4[:, :, 2, :]).then_inc(*inc("gs", 1))
                gs_o = C["gs"]

                @block.sync
                def _(s, o=o, p=p):
                    if xstrF[p] is None:
                        s.wait_ge(SM["ve"], ve_preS3)
                        s.wait_ge(SM["ac"], ac_preS3)
                    else:
                        s.wait_ge(SM["ve"], xstrF[p])
                    s.dma_start(out=f32buf[:, p * TH:(p + 1) * TH],
                                in_=x_fm[o * 128:(o + 1) * 128,
                                         hf * TH:(hf + 1) * TH])\
                        .then_inc(*inc("al"))
                al_o = C["al"]

                def lhs_o(d, wbase=wbase):
                    if d < 8:
                        return wslab[:, wbase + d * 512:wbase + d * 512 + 128]
                    return wslab[:, wbase + 3072 + d * 128:
                                 wbase + 3072 + (d + 1) * 128]

                def rhs_o(d, tc):
                    if d < 8:
                        return scr[:, d * TH + tc * 512:d * TH + (tc + 1) * 512]
                    if d < 40:
                        dd = d - 8
                        return actT[:, dd * TH + tc * 512:dd * TH + (tc + 1) * 512]
                    return hw[:, tc * 512:(tc + 1) * 512]

                def evac_o(tc, ps, mmv, o=o, p=p, al_o=al_o):
                    ob = 2048 + p * TH
                    @block.scalar
                    def _(sc, tc=tc, mmv=mmv, o=o, p=p, ob=ob):
                        sc.wait_ge(SM["mm"], mmv)
                        if outF[p] is not None:
                            sc.wait_ge(SM["xs"], outF[p])
                        else:
                            sc.wait_ge(SM["xs"], C["xs"])
                        sc.activation(
                            f32buf[:, ob + tc * 512:ob + (tc + 1) * 512],
                            ps[:, :], AF.Identity, bias=bias_sb[:, 8 + o:9 + o])\
                            .then_inc(*inc("ac", 1))
                    ac_o = C["ac"]

                    @block.vector
                    def _(v, tc=tc, ac_o=ac_o, al_o=al_o, p=p, ob=ob):
                        v.wait_ge(SM["ac"], ac_o)
                        v.wait_ge(SM["al"], al_o)
                        v.tensor_add(
                            f32buf[:, ob + tc * 512:ob + (tc + 1) * 512],
                            f32buf[:, ob + tc * 512:ob + (tc + 1) * 512],
                            f32buf[:, p * TH + tc * 512:p * TH + (tc + 1) * 512])
                        v.tensor_copy(
                            atw[:, tc * 512:(tc + 1) * 512],
                            f32buf[:, ob + tc * 512:ob + (tc + 1) * 512])
                        v.tensor_mul(
                            qdbd_sb[:, tc * 512:(tc + 1) * 512],
                            f32buf[:, ob + tc * 512:ob + (tc + 1) * 512],
                            f32buf[:, ob + tc * 512:ob + (tc + 1) * 512])\
                            .then_inc(*inc("ve", 1))

                def wait_o(t, wl_o=wl_o, gs_o=gs_o, p=p):
                    t.wait_ge(SM["wl"], wl_o)
                    t.wait_ge(SM["gs"], gs_o)
                    if bankF[p] is None:
                        t.wait_ge(SM["ac"], C["ac"])
                        t.wait_ge(SM["ve"], ve_s3g)
                    else:
                        t.wait_ge(SM["ac"], bankF[p][0])
                        t.wait_ge(SM["ve"], bankF[p][1])

                stops = chain(41, lhs_o, rhs_o, evac_o,
                              banks=(0, 1) if p == 0 else (2, 3),
                              wait_before=wait_o, wait_at={40: ve_hw})
                slabF[p] = stops[-1]
                bankF[p] = (C["ac"], 0)
                xstrF[p] = C["ve"]
                ve_x1 = C["ve"]

                @block.tensor
                def _(t, o=o, ve_x1=ve_x1):
                    t.wait_ge(SM["ve"], ve_x1)
                    for tc in range(2):
                        t.matmul(Dp[0:4, :],
                                 smal[:, 272 + tc * 4:272 + (tc + 1) * 4],
                                 atw[:, tc * 512:(tc + 1) * 512],
                                 start=(o == 0 and tc == 0),
                                 stop=(o == 31 and tc == 1))
                        ins = t.matmul(Bp[0][0:4, :],
                                       smal[:, 272 + tc * 4:272 + (tc + 1) * 4],
                                       qdbd_sb[:, tc * 512:(tc + 1) * 512],
                                       start=(o == 0 and tc == 0),
                                       stop=(o == 31 and tc == 1))
                    ins.then_inc(*inc("mm", 1))

                @block.sync
                def _(s, o=o, p=p, ve_x1=ve_x1):
                    s.wait_ge(SM["ve"], ve_x1)
                    s.dma_start(out=x1_dram[o][:, hf * TH:(hf + 1) * TH],
                                in_=f32buf[:, 2048 + p * TH:2048 + (p + 1) * TH])\
                        .then_inc(*inc("xs"))
                outF[p] = C["xs"]

            st_mm = C["mm"]

            @block.vector
            def _(v, hf=hf, st_mm=st_mm):
                v.wait_ge(SM["mm"], st_mm)
                if hf == 1:
                    v.wait_ge(SM["xs"], stXs[0])
                v.tensor_copy(bcast[0:2, 0:512], Dp[0:2, :])
                v.tensor_copy(bcast[0:2, 512:1024], Bp[0][0:2, :])\
                    .then_inc(*inc("ve", 1))
            ve_fin = C["ve"]

            @block.sync
            def _(s, hf=hf, ve_fin=ve_fin):
                s.wait_ge(SM["ve"], ve_fin)
                s.dma_start(out=st_dram[hf], in_=bcast[0:2, 0:1024])\
                    .then_inc(*inc("xs"))
            stXs[hf] = C["xs"]

        stXs = {}
        run_half_l0(0)
        run_half_l0(1)

        @block.sync
        def _(s):
            s.wait_ge(SM["xs"], C["xs"])
            for hfx in range(2):
                for tc in range(2):
                    s.dma_start(
                        out=rowbuf[0:1, hfx * 1024 + tc * 512:
                                   hfx * 1024 + (tc + 1) * 512],
                        in_=st_dram[hfx][tc:tc + 1, 0:512])\
                        .then_inc(*inc("al"))
                    s.dma_start(
                        out=rowbuf[0:1, T + hfx * 1024 + tc * 512:
                                   T + hfx * 1024 + (tc + 1) * 512],
                        in_=st_dram[hfx][tc:tc + 1, 512:1024])\
                        .then_inc(*inc("al"))

        @block.vector
        def _(v):
            wall(v, "mm")
            wall(v, "al")
            v.tensor_scalar_mul(rowbuf[0:1, 0:T], rowbuf[0:1, 0:T], 1.0 / D)
            v.tensor_scalar_mul(rowbuf[0:1, T:2 * T], rowbuf[0:1, T:2 * T],
                                1.0 / D)
            v.tensor_mul(rowbuf[0:1, 2 * T:3 * T], rowbuf[0:1, 0:T],
                         rowbuf[0:1, 0:T])
            v.tensor_sub(rowbuf[0:1, T:2 * T], rowbuf[0:1, T:2 * T],
                         rowbuf[0:1, 2 * T:3 * T]).then_inc(*inc("ve", 1))
        ve_v = C["ve"]

        @block.scalar
        def _(sc):
            sc.wait_ge(SM["ve"], ve_v)
            sc.activation(rowbuf[0:1, T:2 * T], rowbuf[0:1, T:2 * T],
                          AF.Sqrt, bias=bias_sb[0:1, 80:81]).then_inc(*inc("ac", 1))
        ac_v = C["ac"]

        @block.vector
        def _(v):
            v.wait_ge(SM["ac"], ac_v)
            v.reciprocal(rowbuf[0:1, T:2 * T], rowbuf[0:1, T:2 * T])
            v.tensor_mul(rowbuf[0:1, 2 * T:3 * T], rowbuf[0:1, 0:T],
                         rowbuf[0:1, T:2 * T])
            v.tensor_scalar_mul(rowbuf[0:1, 2 * T:3 * T],
                                rowbuf[0:1, 2 * T:3 * T], -1.0)\
                .then_inc(*inc("ve", 1))
        ve_rs = C["ve"]

        def run_half_l1(hf):
            @block.tensor
            def _(t):
                t.wait_ge(SM["ve"], ve_rs)
                t.wait_ge(SM["ve"], C["ve"])
                for piece in range(2):
                    t.matmul(A[piece][:, :], onesr[0:1, :],
                             rowbuf[0:1, T + hf * TH + piece * 512:
                                    T + hf * TH + (piece + 1) * 512],
                             start=True, stop=True)
                    t.matmul(A[2 + piece][:, :], onesr[0:1, :],
                             rowbuf[0:1, 2 * T + hf * TH + piece * 512:
                                    2 * T + hf * TH + (piece + 1) * 512],
                             start=True, stop=True).then_inc(*inc("mm", 1))
            mm_bc = C["mm"]

            @block.vector
            def _(v, mm_bc=mm_bc):
                v.wait_ge(SM["mm"], mm_bc)
                for piece in range(2):
                    v.tensor_copy(bcast[:, piece * 512:(piece + 1) * 512],
                                  A[piece][:, :])
                v.tensor_copy(bcast[:, TH:TH + 512], A[2][:, :])
                v.tensor_copy(bcast[:, TH + 512:2 * TH], A[3][:, :])\
                    .then_inc(*inc("ve", 1))

            for o in range(32):
                @block.sync
                def _(s, o=o):
                    s.wait_ge(SM["ve"], C["ve"])
                    s.dma_start(out=f32buf[:, 0:TH],
                                in_=x1_dram[o][:, hf * TH:(hf + 1) * TH])\
                        .then_inc(*inc("al"))
                al_n = C["al"]

                @block.vector
                def _(v, o=o, al_n=al_n):
                    v.wait_ge(SM["al"], al_n)
                    v.wait_ge(SM["mm"], C["mm"])
                    v.tensor_mul(f32buf[:, TH:2 * TH], f32buf[:, 0:TH],
                                 bcast[:, 0:TH])
                    v.tensor_add(actT[:, o * TH:(o + 1) * TH],
                                 f32buf[:, TH:2 * TH], bcast[:, TH:2 * TH])\
                        .then_inc(*inc("ve", 1))

            qbankF = None
            kbankF = None
            qdbdldF = None
            slabPartF = [None, None, None]
            wl_part = {}
            tab_rec = {}
            scanEnd = {}
            headPre = {}
            attnpSt = {}
            scaleVe = {}
            stops_rec = {}

            def load_qkv_part(hh, qk):
                @block.gpsimd
                def _(g, hh=hh, qk=qk):
                    if slabPartF[qk] is None:
                        g.wait_ge(SM["mm"], C["mm"])
                    else:
                        g.wait_ge(SM["mm"], slabPartF[qk])
                    g.dma_start(
                        out=wslab[:, qk * 4096:(qk + 1) * 4096]
                        .rearrange("p (c m) -> p c m", c=32),
                        in_=qkv_slabs[hh][qk]
                        .rearrange("(c p) m -> p c m", p=128))\
                        .then_inc(*inc("wl"))
                wl_part[(hh, qk)] = C["wl"]

            def load_tables(hh):
                tp2 = hh % 2
                sv = scanEnd.get(hh - 2)

                @block.sync
                def _(s, hh=hh, tp2=tp2, sv=sv):
                    if sv is not None:
                        s.wait_ge(SM["ve"], sv)
                    else:
                        s.wait_ge(SM["ve"], C["ve"])
                    s.dma_start(out=diag_sb[:, tp2 * 512:(tp2 + 1) * 512]
                                .rearrange("p (c i) -> p c i", c=2),
                                in_=diagT[hh].rearrange("c p i -> p c i"))\
                        .then_inc(*inc("al"))
                    s.dma_start(out=qd_sb[:, tp2 * BLK:(tp2 + 1) * BLK],
                                in_=qd_tab[hh]).then_inc(*inc("al"))
                    if qdbdldF is not None:
                        s.wait_ge(SM["ve"], qdbdldF)
                    s.dma_start(out=qdbd_sb[:, :],
                                in_=qdbd_tab[hh][:, hf * TH:(hf + 1) * TH])\
                        .then_inc(*inc("al"))
                    if hf == 1:
                        s.wait_ge(SM["gs"], C["gs"])
                        s.dma_start(out=kv_sb[:, tp2 * 128:(tp2 + 1) * 128],
                                    in_=s_src[hh * 128:(hh + 1) * 128, :])\
                            .then_inc(*inc("al"))
                tab_rec[hh] = C["al"]

            def slot(hh, which):
                return scr[:, ((hh % 2) * 4 + which) * TH:
                           ((hh % 2) * 4 + which + 1) * TH]

            def attnp_buf(hh):
                return attnp_sb[:, :] if hh % 2 == 0 else f32buf[:, 0:TH]

            def emit_subchain(hh, qk, lo, hi):
                nonlocal qbankF, kbankF
                bks = (0, 1) if qk != 1 else (2, 3)

                @block.tensor
                def _(t, hh=hh, qk=qk, lo=lo, hi=hi, bks=bks):
                    if lo == 0:
                        t.wait_ge(SM["wl"], wl_part[(hh, qk)])
                        bf = [qbankF, kbankF, qbankF][qk]
                        if bf is None:
                            t.wait_ge(SM["ac"], C["ac"])
                            t.wait_ge(SM["ve"], C["ve"])
                        else:
                            t.wait_ge(SM["ac"], bf)
                    for d in range(lo, hi):
                        for tc in range(2):
                            ins = t.matmul(
                                A[bks[tc]][:, :],
                                wslab[:, qk * 4096 + d * 128:
                                      qk * 4096 + (d + 1) * 128],
                                actT[:, d * TH + tc * 512:
                                     d * TH + (tc + 1) * 512],
                                start=(d == 0), stop=(d == 31))
                            if d == 31:
                                ins.then_inc(*inc("mm", 1))
                                stops_rec.setdefault((hh, qk), []).append(
                                    C["mm"])
                if hi == 32:
                    sv = scanEnd.get(hh - 2)
                    for tc in range(2):
                        mmv = stops_rec[(hh, qk)][tc]

                        @block.scalar
                        def _(sc, tc=tc, mmv=mmv, hh=hh, qk=qk, sv=sv):
                            sc.wait_ge(SM["mm"], mmv)
                            if tc == 0:
                                if sv is not None:
                                    sc.wait_ge(SM["ve"], sv)
                                else:
                                    sc.wait_ge(SM["ve"], C["ve"])
                            sc.activation(
                                slot(hh, qk)[:, tc * 512:(tc + 1) * 512],
                                A[bks[tc]][:, :], AF.Silu)\
                                .then_inc(*inc("ac", 1))
                    if qk == 1:
                        kbankF = C["ac"]
                    else:
                        qbankF = C["ac"]
                    slabPartF[qk] = stops_rec[(hh, qk)][-1]

            def emit_head_start(hh):
                nonlocal qdbdldF
                tp2 = hh % 2
                ac_qkv = C["ac"]
                kvr = kv_sb[:, tp2 * 128:(tp2 + 1) * 128]
                kvbr = kvb_sb[:, tp2 * 128:(tp2 + 1) * 128]
                stp = attnpSt.get(hh - 2)

                @block.vector
                def _(v, hh=hh, ac_qkv=ac_qkv, kvr=kvr, kvbr=kvbr, stp=stp):
                    v.wait_ge(SM["ac"], ac_qkv)
                    if stp is not None:
                        v.wait_ge(SM["xs"], stp)
                    v.wait_ge(SM["al"], tab_rec[hh])
                    if hf == 0:
                        v.wait_ge(SM["gs"], C["gs"])
                        v.memset(kvr, 0.0)
                    v.tensor_copy(kvbr, kvr)
                    v.tensor_mul(slot(hh, 3), slot(hh, 0), qdbd_sb[:, :])\
                        .then_inc(*inc("ve", 1))
                headPre[hh] = C["ve"]
                qdbdldF = C["ve"]

                @block.sync
                def _(s, hh=hh):
                    s.wait_ge(SM["ve"], headPre[hh])
                    s.dma_start(out=qdbd_out[hh][:, hf * TH:(hf + 1) * TH],
                                in_=slot(hh, 3)).then_inc(*inc("xs"))

            scr8 = 8 * TH

            def atw_reg(part):
                if part % 2 == 0:
                    return (atw[:, 0:256], atw[:, 256:768],
                            atw[:, 768:1024], atw[:, 1024:1280])
                return (mz_sb[:, 0:256], scr[:, scr8:scr8 + 512],
                        scr[:, scr8 + 512:scr8 + 768],
                        scr[:, scr8 + 768:scr8 + 1024])

            def emit_stageA(hh, part):
                base = part * BLK
                cpo = (part % 2) * 512
                SB = Bp[0] if part % 2 == 0 else Dp
                ks, vs, qs = slot(hh, 1), slot(hh, 2), slot(hh, 0)
                wv = scaleVe.get(part % 2)

                @block.tensor
                def _(t, hh=hh, part=part, base=base, cpo=cpo, SB=SB,
                      ks=ks, vs=vs, qs=qs, wv=wv):
                    if part == 0:
                        t.wait_ge(SM["ac"], C["ac"])
                    if wv is None:
                        t.wait_ge(SM["ve"], C["ve"])
                    else:
                        t.wait_ge(SM["ve"], wv)
                    for c2 in range(2):
                        t.transpose(Cp[:, cpo + c2 * 128:cpo + (c2 + 1) * 128],
                                    ks[:, base + c2 * 128:base + (c2 + 1) * 128],
                                    smal[:, 0:128])
                    for c2 in range(2):
                        ins = t.transpose(
                            Cp[:, cpo + 256 + c2 * 128:cpo + 256 + (c2 + 1) * 128],
                            vs[:, base + c2 * 128:base + (c2 + 1) * 128],
                            smal[:, 0:128])
                    ins.then_inc(*inc("mm", 1))
                    for c2 in range(2):
                        ins = t.matmul(SB[:, c2 * 256:(c2 + 1) * 256],
                                       ks[:, base + c2 * 128:base + (c2 + 1) * 128],
                                       qs[:, base:base + 256],
                                       start=True, stop=True)
                    ins.then_inc(*inc("mm", 1))
                return C["mm"]

            def emit_scale(hh, part, mm_tv):
                tp2 = hh % 2
                jg = hf * 4 + part
                base = part * BLK
                cpo = (part % 2) * 512
                SB = Bp[0] if part % 2 == 0 else Dp
                kT, scb, qdb_r, vT = atw_reg(part)
                qs = slot(hh, 0)

                @block.vector
                def _(v, hh=hh, jg=jg, base=base, mm_tv=mm_tv, tp2=tp2,
                      cpo=cpo, SB=SB, kT=kT, scb=scb, qdb_r=qdb_r, vT=vT,
                      qs=qs):
                    v.wait_ge(SM["mm"], mm_tv)
                    for c2 in range(2):
                        v.tensor_single_scalar(
                            kT[:, c2 * 128:(c2 + 1) * 128],
                            Cp[:, cpo + c2 * 128:cpo + (c2 + 1) * 128],
                            colbuf[:, hh * 2 + c2:hh * 2 + c2 + 1], ALU.mult)
                        v.tensor_copy(
                            vT[:, c2 * 128:(c2 + 1) * 128],
                            Cp[:, cpo + 256 + c2 * 128:cpo + 256 + (c2 + 1) * 128])
                    for c2 in range(2):
                        v.tensor_mul(
                            scb[:, c2 * 256:(c2 + 1) * 256],
                            SB[:, c2 * 256:(c2 + 1) * 256],
                            diag_sb[:, tp2 * 512 + c2 * 256:
                                    tp2 * 512 + (c2 + 1) * 256])
                    if jg > 0:
                        v.tensor_mul(qdb_r[:, 0:256],
                                     qs[:, base:base + 256],
                                     qd_sb[:, tp2 * BLK:(tp2 + 1) * BLK])\
                            .then_inc(*inc("ve", 1))
                    else:
                        v.tensor_copy(qdb_r[:, 0:8], qdb_r[:, 0:8])\
                            .then_inc(*inc("ve", 1))
                scaleVe[part % 2] = C["ve"]
                return C["ve"]

            def emit_stageB(hh, part, ve_ds):
                jg = hf * 4 + part
                tp2 = hh % 2
                kvbr = kvb_sb[:, tp2 * 128:(tp2 + 1) * 128]
                kT, scb, qdb_r, vT = atw_reg(part)

                @block.tensor
                def _(t, jg=jg, ve_ds=ve_ds, kvbr=kvbr, kT=kT, scb=scb,
                      qdb_r=qdb_r, vT=vT):
                    t.wait_ge(SM["ve"], ve_ds)
                    nmm = 3 if jg > 0 else 2
                    kk = 0
                    for c2 in range(2):
                        kk += 1
                        ins = t.matmul(
                            Bp[1][:, 0:256],
                            vT[:, c2 * 128:(c2 + 1) * 128],
                            scb[:, c2 * 256:(c2 + 1) * 256],
                            start=(kk == 1), stop=(kk == nmm))
                    if jg > 0:
                        ins = t.matmul(Bp[1][:, 0:256], kvbr,
                                       qdb_r[:, 0:256],
                                       start=False, stop=True)
                    ins.then_inc(*inc("mm", 1))
                    for c2 in range(2):
                        ins = t.matmul(
                            Bp[1][:, 256:384],
                            kT[:, c2 * 128:(c2 + 1) * 128],
                            vT[:, c2 * 128:(c2 + 1) * 128],
                            start=(c2 == 0), stop=(c2 == 1))
                    ins.then_inc(*inc("mm", 1))
                return C["mm"]

            def emit_kv(hh, part, mm_kv):
                tp2 = hh % 2
                base = part * BLK
                kvr = kv_sb[:, tp2 * 128:(tp2 + 1) * 128]
                kvbr = kvb_sb[:, tp2 * 128:(tp2 + 1) * 128]
                apb = attnp_buf(hh)

                @block.vector
                def _(v, hh=hh, base=base, mm_kv=mm_kv, kvr=kvr, kvbr=kvbr,
                      apb=apb):
                    v.wait_ge(SM["mm"], mm_kv)
                    v.tensor_copy(apb[:, base:base + 256], Bp[1][:, 0:256])
                    vF(v, v.scalar_tensor_tensor(kvr, kvr, BD_TAB[hh],
                                                 Bp[1][:, 256:384],
                                                 ALU.mult, ALU.add))
                    v.tensor_copy(kvbr, kvr).then_inc(*inc("ve", 1))

            def emit_scan_end(hh):
                scanEnd[hh] = C["ve"]
                ve_head = C["ve"]
                tp2 = hh % 2
                kvr = kv_sb[:, tp2 * 128:(tp2 + 1) * 128]
                apb = attnp_buf(hh)

                @block.sync
                def _(s, hh=hh, ve_head=ve_head, apb=apb):
                    s.wait_ge(SM["ve"], ve_head)
                    s.dma_start(out=attnp_out[hh][:, hf * TH:(hf + 1) * TH],
                                in_=apb).then_inc(*inc("xs"))
                attnpSt[hh] = C["xs"]

                @block.gpsimd
                def _(g, hh=hh, ve_head=ve_head, kvr=kvr):
                    g.wait_ge(SM["ve"], ve_head)
                    g.dma_start(out=s_src[hh * 128:(hh + 1) * 128, :],
                                in_=kvr).then_inc(*inc("gs"))

            for qk in range(3):
                load_qkv_part(0, qk)
            load_tables(0)

            def scan_pair(sh, part):
                mm_tv = emit_stageA(sh, part)
                emit_scale(sh, part, mm_tv)

            def scan_fin(sh, part):
                ve_ds = scaleVe[part % 2]
                mm_kv = emit_stageB(sh, part, ve_ds)
                emit_kv(sh, part, mm_kv)

            for h in range(H1 + 1):
                sh = h - 1
                ch = h < H1
                if sh >= 0:
                    scan_pair(sh, 0)
                if ch:
                    emit_subchain(h, 0, 0, 16)
                if sh >= 0:
                    scan_fin(sh, 0)
                    scan_pair(sh, 1)
                if ch:
                    emit_subchain(h, 0, 16, 32)
                    if h + 1 < H1:
                        load_qkv_part(h + 1, 0)
                if sh >= 0:
                    scan_fin(sh, 1)
                    scan_pair(sh, 2)
                if ch:
                    emit_subchain(h, 1, 0, 16)
                if sh >= 0:
                    scan_fin(sh, 2)
                    scan_pair(sh, 3)
                if ch:
                    emit_subchain(h, 1, 16, 32)
                    if h + 1 < H1:
                        load_qkv_part(h + 1, 1)
                if sh >= 0:
                    scan_fin(sh, 3)
                if ch:
                    emit_subchain(h, 2, 0, 16)
                if sh >= 0:
                    emit_scan_end(sh)
                    if hf == 1 and sh in (15, 31):
                        gs_c = C["gs"]
                        csem = "c1" if sh == 15 else "c3"
                        ssl = s_src[0:16 * 128, :] if sh == 15 \
                            else s_src[16 * 128:32 * 128, :]
                        sout = sall1 if sh == 15 else sall2

                        @block.gpsimd
                        def _(g, gs_c=gs_c, csem=csem, ssl=ssl, sout=sout):
                            g.wait_ge(SM["gs"], gs_c)
                            g.collective_compute(
                                "AllGather", ALU.bypass,
                                replica_groups=[[0, 1, 2, 3], [4, 5, 6, 7]],
                                ins=[ssl.opt()],
                                outs=[sout.opt()]).then_inc(SM[csem], 1)
                if ch:
                    emit_subchain(h, 2, 16, 32)
                    if h + 1 < H1:
                        load_qkv_part(h + 1, 2)
                    emit_head_start(h)
                    if h + 1 < H1:
                        load_tables(h + 1)

            slabF = [None, None]
            gbankF = None
            hbankF = None
            gstgF = [None, None]
            xstrF = [None, None]
            houtF = [None, None]
            for o in range(32):
                p = o % 2
                wbase = p * 8320

                @block.gpsimd
                def _(g, o=o, p=p, wbase=wbase):
                    if slabF[p] is None:
                        g.wait_ge(SM["mm"], C["mm"])
                    else:
                        g.wait_ge(SM["mm"], slabF[p])
                    g.dma_start(
                        out=wslab[:, wbase:wbase + 8192]
                        .rearrange("p (c m) -> p c m", c=64),
                        in_=gw1_slabs[o].rearrange("(c p) m -> p c m", p=128))\
                        .then_inc(*inc("wl"))
                wl_o = C["wl"]

                @block.sync
                def _(s, o=o, p=p):
                    if xstrF[p] is None:
                        s.wait_ge(SM["ve"], C["ve"])
                        s.wait_ge(SM["xs"], C["xs"])
                    else:
                        s.wait_ge(SM["ve"], xstrF[p])
                    s.dma_start(out=f32buf[:, p * TH:(p + 1) * TH],
                                in_=x1_dram[o][:, hf * TH:(hf + 1) * TH])\
                        .then_inc(*inc("al"))
                al_o = C["al"]

                def evac_gt(tc, ps, mmv, o=o, p=p):
                    @block.scalar
                    def _(sc, tc=tc, mmv=mmv, p=p):
                        sc.wait_ge(SM["mm"], mmv)
                        if gstgF[p] is not None:
                            sc.wait_ge(SM["xs"], gstgF[p])
                        else:
                            sc.wait_ge(SM["xs"], C["xs"])
                            sc.wait_ge(SM["ve"], C["ve"])
                        sc.activation(scr[:, p * TH + tc * 512:
                                          p * TH + (tc + 1) * 512],
                                      ps[:, :], AF.Sigmoid).then_inc(*inc("ac", 1))

                def wait_gt(t, wl_o=wl_o):
                    nonlocal gbankF
                    t.wait_ge(SM["wl"], wl_o)
                    if gbankF is None:
                        t.wait_ge(SM["ac"], C["ac"])
                        t.wait_ge(SM["ve"], C["ve"])
                    else:
                        t.wait_ge(SM["ac"], gbankF)

                chain(32, lambda d, wbase=wbase: wslab[:, wbase + d * 128:
                                                       wbase + (d + 1) * 128],
                      lambda d, tc: actT[:, d * TH + tc * 512:
                                         d * TH + (tc + 1) * 512],
                      evac_gt, banks=(0, 1), wait_before=wait_gt)
                gbankF = C["ac"]
                ac_g = C["ac"]

                @block.sync
                def _(s, o=o, p=p, ac_g=ac_g):
                    s.wait_ge(SM["ac"], ac_g)
                    s.dma_start(out=gate_out[o][:, hf * TH:(hf + 1) * TH],
                                in_=scr[:, p * TH:(p + 1) * TH])\
                        .then_inc(*inc("xs"))
                gstgF[p] = C["xs"]

                def evac_h(tc, ps, mmv, o=o, p=p, al_o=al_o):
                    ob = 2048 + p * TH
                    @block.scalar
                    def _(sc, tc=tc, mmv=mmv, o=o, p=p, ob=ob):
                        sc.wait_ge(SM["mm"], mmv)
                        if houtF[p] is not None:
                            sc.wait_ge(SM["xs"], houtF[p])
                        else:
                            sc.wait_ge(SM["xs"], C["xs"])
                        sc.activation(
                            f32buf[:, ob + tc * 512:ob + (tc + 1) * 512],
                            ps[:, :], AF.Identity, bias=bias_sb[:, 40 + o:41 + o])\
                            .then_inc(*inc("ac", 1))
                    ac_h = C["ac"]

                    @block.vector
                    def _(v, tc=tc, ac_h=ac_h, o=o, p=p, al_o=al_o, ob=ob):
                        v.wait_ge(SM["ac"], ac_h)
                        v.wait_ge(SM["al"], al_o)
                        if o == 0:
                            v.tensor_copy(
                                hs_sb[0:8, hf * 512 + tc * 256:
                                      hf * 512 + (tc + 1) * 256],
                                f32buf[0:8, ob + tc * 512 + 1:
                                       ob + (tc + 1) * 512:2])
                        v.tensor_add(
                            f32buf[:, ob + tc * 512:ob + (tc + 1) * 512],
                            f32buf[:, ob + tc * 512:ob + (tc + 1) * 512],
                            f32buf[:, p * TH + tc * 512:p * TH + (tc + 1) * 512])\
                            .then_inc(*inc("ve", 1))

                def wait_h(t):
                    nonlocal hbankF
                    if hbankF is None:
                        t.wait_ge(SM["ac"], C["ac"])
                        t.wait_ge(SM["ve"], C["ve"])
                    else:
                        t.wait_ge(SM["ac"], hbankF[0])
                        t.wait_ge(SM["ve"], hbankF[1])

                chain(32, lambda d, wbase=wbase: wslab[:, wbase + 4096 + d * 128:
                                                       wbase + 4096 + (d + 1) * 128],
                      lambda d, tc: actT[:, d * TH + tc * 512:
                                         d * TH + (tc + 1) * 512],
                      evac_h, banks=(2, 3), wait_before=wait_h)
                slabF[p] = C["mm"]
                hbankF = (C["ac"], C["ve"])
                xstrF[p] = C["ve"]
                ve_h = C["ve"]

                @block.sync
                def _(s, o=o, p=p, ve_h=ve_h):
                    s.wait_ge(SM["ve"], ve_h)
                    s.dma_start(out=r_out[o][:, hf * TH:(hf + 1) * TH],
                                in_=f32buf[:, 2048 + p * TH:2048 + (p + 1) * TH])\
                        .then_inc(*inc("xs"))
                houtF[p] = C["xs"]

            @block.gpsimd
            def _(g):
                g.wait_ge(SM["mm"], C["mm"])
                g.dma_start(out=wslab[:, 0:1024]
                            .rearrange("p (c m) -> p c m", c=32),
                            in_=a_l1.rearrange("(c p) m -> p c m", p=128))\
                    .then_inc(*inc("wl"))
            wl_z = C["wl"]

            for tc in range(2):
                @block.tensor
                def _(t, tc=tc):
                    t.wait_ge(SM["wl"], wl_z)
                    t.wait_ge(SM["ve"], C["ve"])
                    for d in range(32):
                        ins = t.matmul(Bp[0][0:32, :],
                                       wslab[:, d * 32:(d + 1) * 32],
                                       actT[:, d * TH + tc * 512:
                                            d * TH + (tc + 1) * 512],
                                       start=(d == 0), stop=(d == 31))
                    ins.then_inc(*inc("mm", 1))
                mm_z = C["mm"]

                @block.vector
                def _(v, tc=tc, mm_z=mm_z):
                    v.wait_ge(SM["mm"], mm_z)
                    v.tensor_copy(zt_sb[0:32, hf * TH + tc * 512:
                                        hf * TH + (tc + 1) * 512],
                                  Bp[0][0:32, :]).then_inc(*inc("ve", 1))

        run_half_l1(0)
        run_half_l1(1)

        @block.vector
        def _(v):
            wall(v, "ve", "mm")
            v.tensor_copy(scr[0:32, 0:T], zt_sb[0:32, :])\
                .then_inc(*inc("ve", 1))
        ve_a = C["ve"]

        for tc in range(2):
            @block.tensor
            def _(t, tc=tc):
                t.wait_ge(SM["ve"], ve_a)
                t.wait_ge(SM["ve"], C["ve"])
                ins = t.matmul(Bp[1][0:8, :], smal[0:32, 257:265],
                               scr[0:32, 1 + 2 * tc * 512:
                                   1 + 2 * (tc + 1) * 512:2],
                               start=True, stop=True)
                ins.then_inc(*inc("mm", 1))
            mm_d = C["mm"]

            @block.vector
            def _(v, tc=tc, mm_d=mm_d):
                v.wait_ge(SM["mm"], mm_d)
                v.tensor_copy(hs_sb[0:8, 1024 + tc * 512:
                                    1024 + (tc + 1) * 512],
                              Bp[1][0:8, :]).then_inc(*inc("ve", 1))

        @block.vector
        def _(v):
            wall(v, "ve")
            v.reduce_max(colbuf[0:8, 84:85], hs_sb[0:8, 0:1024], AX,
                         apply_absolute_value=True)
            v.reduce_max(colbuf[0:8, 85:86], hs_sb[0:8, 1024:2048], AX,
                         apply_absolute_value=True).then_inc(*inc("ve", 1))
        ve_mx = C["ve"]

        @block.sync
        def _(s):
            s.wait_ge(SM["ve"], ve_mx)
            s.dma_start(out=mxb[0:8, :], in_=colbuf[0:8, 84:85])\
                .then_inc(*inc("xs"))
            s.dma_start(out=mxb[8:16, :], in_=colbuf[0:8, 85:86])\
                .then_inc(*inc("xs"))
        xs_mx = C["xs"]

        @block.sync
        def _(s):
            s.wait_ge(SM["xs"], xs_mx)
            s.dma_start(out=mrow[0:1, 0:16], in_=mxb.rearrange("a b -> b a"))\
                .then_inc(*inc("al"))
        al_mx = C["al"]

        @block.vector
        def _(v):
            v.wait_ge(SM["al"], al_mx)
            v.reduce_max(mrow[0:1, 16:17], mrow[0:1, 0:8], AX)
            v.reduce_max(mrow[0:1, 17:18], mrow[0:1, 8:16], AX)\
                .then_inc(*inc("ve", 1))
        ve_mr = C["ve"]

        @block.sync
        def _(s):
            s.wait_ge(SM["ve"], ve_mr)
            s.dma_start(out=mx_src.rearrange("a b -> b a"),
                        in_=mrow[0:1, 16:18]).then_inc(*inc("xs"))

        @block.gpsimd
        def _(g):
            g.wait_ge(SM["xs"], C["xs"])
            g.wait_ge(SM["gs"], C["gs"])
            g.collective_compute(
                "AllGather", ALU.bypass,
                replica_groups=[[0, 1, 2, 3, 4, 5, 6, 7]],
                ins=[mx_src.opt()], outs=[mxg.opt()]).then_inc(SM["c2"], 1)

        fence_at = dict(C)

        def fence(eng):
            for n in ["wl", "al", "xb", "mm", "ac", "ve", "gs", "xs"]:
                if fence_at[n] > 0:
                    eng.wait_ge(SM[n], fence_at[n])
            eng.wait_ge(SM["c1"], 1)
            eng.wait_ge(SM["c2"], 1)
            eng.wait_ge(SM["c3"], 1)

        @block.sync
        def _(s):
            fence(s)

        @block.gpsimd
        def _(g):
            fence(g)

        @block.vector
        def _(v):
            fence(v)

        @block.scalar
        def _(sc):
            fence(sc)

        @block.tensor
        def _(t):
            fence(t)

        gnT = actT
        kvin = scr
        qdb = scr
        f32a = f32buf
        R2 = bcast
        gatef = attnp_sb
        sprd = diag_sb

        @block.gpsimd
        def _(g):
            g.dma_start(out=wslab[0:32, 8192:12288], in_=bmat)\
                .then_inc(*inc("wl"))
            for o in range(2):
                g.dma_start(
                    out=wslab[:, o * 4096:(o + 1) * 4096]
                    .rearrange("p (c m) -> p c m", c=32),
                    in_=outw_slabs[o].rearrange("(c p) m -> p c m", p=128))\
                    .then_inc(*inc("wl"))
        pre_wl = {0: C["wl"] - 16, 1: C["wl"]}

        @block.sync
        def _(s):
            s.dma_start(out=mrow[0:1, 0:16], in_=mxg.rearrange("a b -> b a"))\
                .then_inc(*inc("al"))
        al_gx = C["al"]

        @block.vector
        def _(v):
            v.wait_ge(SM["al"], al_gx)
            vF(v, v.reduce_max(mrow[0:1, 18:19], mrow[0:1, 0:16:2], AX))
            vF(v, v.reduce_max(mrow[0:1, 19:20], mrow[0:1, 1:16:2], AX))
            vF(v, v.reciprocal(mrow[0:1, 20:22], mrow[0:1, 18:20]))
            v.tensor_scalar_mul(mrow[0:1, 20:22], mrow[0:1, 20:22], 0.5)\
                .then_inc(*inc("ve", 1))
        ve_gx = C["ve"]

        @block.tensor
        def _(t):
            t.wait_ge(SM["ve"], ve_gx)
            t.matmul(A[0][:, 0:1], onesr[0:1, :], mrow[0:1, 20:21],
                     start=True, stop=True)
            t.matmul(A[1][:, 0:1], onesr[0:1, :], mrow[0:1, 21:22],
                     start=True, stop=True).then_inc(*inc("mm", 1))
        mm_gx = C["mm"]

        @block.vector
        def _(v):
            v.wait_ge(SM["mm"], mm_gx)
            v.tensor_copy(colbuf[:, 0:1], A[0][:, 0:1])
            v.tensor_copy(colbuf[:, 1:2], A[1][:, 0:1]).then_inc(*inc("ve", 1))
        ve_cb = C["ve"]

        def emit_t2_it(it):
            ve_pv = C["ve"]

            @block.tensor
            def _(t, it=it, ve_pv=ve_pv):
                t.wait_ge(SM["ve"], ve_pv)
                t.transpose(Dp[:, 0:8],
                            hs_sb[0:8, it * 128:(it + 1) * 128],
                            idf[0:8, 0:8])
                t.transpose(Dp[:, 8:16],
                            hs_sb[0:8, 1024 + it * 128:1024 + (it + 1) * 128],
                            idf[0:8, 0:8]).then_inc(*inc("mm", 1))
            mm_tt = C["mm"]

            @block.vector
            def _(v, mm_tt=mm_tt):
                v.wait_ge(SM["mm"], mm_tt)
                vF(v, v.tensor_copy(ltile[:, 0:16], Dp[:, 0:16]))
                v.tensor_single_scalar(ltile[:, 16:24], ltile[:, 0:8],
                                       colbuf[:, 0:1], ALU.mult)
                vF(v, v.tensor_single_scalar(ltile[:, 24:32], ltile[:, 8:16],
                                             colbuf[:, 1:2], ALU.mult))
                vF(v, v.tensor_add(ltile[:, 32:40], ltile[:, 16:24],
                                   ltile[:, 24:32]))
                vF(v, v.reduce_max(colbuf[:, 2:3], ltile[:, 32:40], AX))
                v.tensor_scalar_mul(colbuf[:, 3:4], colbuf[:, 2:3], -1.0)\
                    .then_inc(*inc("ve", 1))
            ve_lg = C["ve"]

            @block.scalar
            def _(sc, ve_lg=ve_lg):
                sc.wait_ge(SM["ve"], ve_lg)
                sc.activation(ltile[:, 40:48], ltile[:, 32:40], AF.Exp,
                              bias=colbuf[:, 3:4]).then_inc(*inc("ac", 1))
            ac_e = C["ac"]

            @block.vector
            def _(v, ac_e=ac_e):
                v.wait_ge(SM["ac"], ac_e)
                vF(v, v.reduce_sum(colbuf[:, 4:5], ltile[:, 40:48], AX))
                vF(v, v.reciprocal(colbuf[:, 5:6], colbuf[:, 4:5]))
                vF(v, v.tensor_single_scalar(ltile[:, 48:56], ltile[:, 40:48],
                                             colbuf[:, 5:6], ALU.mult))
                vF(v, v.reduce_max(colbuf[:, 6:7], ltile[:, 48:56], AX))
                v.tensor_single_scalar(ltile[:, 56:64], ltile[:, 48:56],
                                       colbuf[:, 6:7], ALU.subtract)\
                    .then_inc(*inc("ve", 1))
            ve_p = C["ve"]

            @block.scalar
            def _(sc, ve_p=ve_p):
                sc.wait_ge(SM["ve"], ve_p)
                aF(sc, sc.activation(ltile[:, 64:72], ltile[:, 56:64],
                                     AF.Sign))
                sc.activation(ltile[:, 72:80], ltile[:, 64:72], AF.Identity,
                              bias=bias_sb[:, 82:83], scale=bias_sb[:, 82:83])\
                        .then_inc(*inc("ac", 1))
            ac_s = C["ac"]

            @block.vector
            def _(v, ac_s=ac_s):
                v.wait_ge(SM["ac"], ac_s)
                vF(v, v.tensor_sub(ltile[:, 80:88], ltile[:, 48:56],
                                   ltile[:, 72:80]))
                vF(v, v.reduce_max(colbuf[:, 7:8], ltile[:, 80:88], AX))
                v.tensor_single_scalar(ltile[:, 88:96], ltile[:, 48:56],
                                       colbuf[:, 7:8], ALU.subtract)\
                    .then_inc(*inc("ve", 1))
            ve_th = C["ve"]

            @block.scalar
            def _(sc, ve_th=ve_th):
                sc.wait_ge(SM["ve"], ve_th)
                sc.activation(ltile[:, 96:104], ltile[:, 88:96], AF.Sigmoid,
                              scale=2.0).then_inc(*inc("ac", 1))
            ac_m = C["ac"]

            @block.vector
            def _(v, ac_m=ac_m):
                v.wait_ge(SM["ac"], ac_m)
                vF(v, v.tensor_mul(ltile[:, 104:112], ltile[:, 48:56],
                                   ltile[:, 96:104]))
                vF(v, v.reduce_sum(colbuf[:, 8:9], ltile[:, 104:112], AX))
                vF(v, v.tensor_scalar_add(colbuf[:, 9:10], colbuf[:, 8:9],
                                          1e-9))
                vF(v, v.reciprocal(colbuf[:, 10:11], colbuf[:, 9:10]))
                vF(v, v.tensor_single_scalar(ltile[:, 112:120],
                                             ltile[:, 104:112],
                                             colbuf[:, 10:11], ALU.mult))
                v.tensor_copy(scr[:, 6656:6664], ltile[:, 112:120])\
                    .then_inc(*inc("ve", 1))
            ve_w = C["ve"]

            @block.tensor
            def _(t, ve_w=ve_w):
                t.wait_ge(SM["ve"], ve_w)
                t.transpose(Cp[0:8, 0:128], scr[:, 6656:6664],
                            smal[:, 0:128]).then_inc(*inc("mm", 1))
            mm_w = C["mm"]

            @block.vector
            def _(v, it=it, mm_w=mm_w):
                v.wait_ge(SM["mm"], mm_w)
                v.tensor_copy(qdbd_sb[0:8, it * 128:(it + 1) * 128],
                              Cp[0:8, 0:128]).then_inc(*inc("ve", 1))

        def emit_t2_tail():
            @block.vector
            def _(v):
                wall(v, "ve")
                v.tensor_copy(lgT[0:8, 0:T:2], qdbd_sb[0:8, 0:T // 2])
                v.tensor_copy(lgT[0:8, 1:T:2], qdbd_sb[0:8, 0:T // 2])\
                    .then_inc(*inc("ve", 1))
            ve_wf = C["ve"]

            for tc in range(4):
                @block.tensor
                def _(t, tc=tc, ve_wf=ve_wf):
                    t.wait_ge(SM["ve"], ve_wf)
                    if tc > 0:
                        t.wait_ge(SM["ve"], C["ve"])
                    t.matmul(Bp[0][0:32, :], limes_sb[0:8, :],
                             lgT[0:8, tc * 512:(tc + 1) * 512],
                             start=True, stop=True).then_inc(*inc("mm", 1))
                mm_p = C["mm"]

                @block.vector
                def _(v, tc=tc, mm_p=mm_p):
                    v.wait_ge(SM["mm"], mm_p)
                    v.scalar_tensor_tensor(scr[0:32, 7168 + tc * 512:
                                               7168 + (tc + 1) * 512],
                                           Bp[0][0:32, :], 0.5,
                                           zt_sb[0:32, tc * 512:(tc + 1) * 512],
                                           ALU.mult, ALU.mult)\
                        .then_inc(*inc("ve", 1))

        kvinVe = {}
        spF = [None, None]
        for h in range(H1):
            ph3 = h % 2
            sbase = ph3 * 512

            @block.gpsimd
            def _(g, h=h, ph3=ph3, sbase=sbase):
                if spF[ph3] is None:
                    g.wait_ge(SM["ve"], C["ve"])
                else:
                    g.wait_ge(SM["ve"], spF[ph3])
                sg = sall1 if h < 16 else sall2
                hh = h % 16
                for sp in range(4):
                    g.dma_start(
                        out=diag_sb[:, sbase + sp * 128:sbase + (sp + 1) * 128],
                        in_=sg[sp * 16 * 128 + hh * 128:
                               sp * 16 * 128 + (hh + 1) * 128, :])\
                        .then_inc(*inc("gs"))
            al_s = C["gs"]

            @block.vector
            def _(v, h=h, sbase=sbase, al_s=al_s):
                v.wait_ge(SM["gs"], al_s)
                kvt = bcast[:, 0:128]
                vF(v, v.tensor_single_scalar(
                    kvt, diag_sb[:, sbase:sbase + 128],
                    coefq[:, h:h + 1], ALU.mult))
                for sp in range(1, 4):
                    vF(v, v.scalar_tensor_tensor(
                        kvt, diag_sb[:, sbase + sp * 128:sbase + (sp + 1) * 128],
                        coefq[:, sp * H1 + h:sp * H1 + h + 1],
                        kvt, ALU.mult, ALU.add))
                v.tensor_copy(kvin[:, h * 128:(h + 1) * 128], kvt)\
                    .then_inc(*inc("ve", 1))
            spF[ph3] = C["ve"]
            kvinVe[h] = C["ve"]

        corrMM = {}
        corrGs = {}
        evacVe = {}
        for h in range(H1):
            ph = h % 2
            fb = ph * 2048
            qb = ph * 2048
            bkp = (A[0], A[1]) if ph == 0 else (A[2], A[3])

            @block.sync
            def _(s, h=h, fb=fb):
                if corrGs.get(h - 2) is not None:
                    s.wait_ge(SM["gs"], corrGs[h - 2])
                else:
                    s.wait_ge(SM["xs"], C["xs"])
                s.dma_start(out=f32buf[:, fb:fb + T], in_=attnp_out[h])\
                    .then_inc(*inc("al"))
            al_c = C["al"]

            @block.scalar
            def _(sc, h=h, qb=qb):
                if corrMM.get(h - 2) is not None:
                    sc.wait_ge(SM["mm"], corrMM[h - 2])
                sc.dma_start(out=actT[:, qb:qb + T], in_=qdbd_out[h])\
                    .then_inc(*inc("xb"))
            xb_c = C["xb"]

            for tc in range(4):
                wv = evacVe.get((ph, tc % 2))

                @block.tensor
                def _(t, h=h, tc=tc, al_c=al_c, xb_c=xb_c, wv=wv, qb=qb,
                      bkp=bkp):
                    if tc == 0:
                        t.wait_ge(SM["al"], al_c)
                        t.wait_ge(SM["xb"], xb_c)
                        t.wait_ge(SM["ve"], kvinVe[h])
                    if wv is None:
                        t.wait_ge(SM["ve"], C["ve"])
                    else:
                        t.wait_ge(SM["ve"], wv)
                    t.matmul(bkp[tc % 2][:, :], kvin[:, h * 128:(h + 1) * 128],
                             actT[:, qb + tc * 512:qb + (tc + 1) * 512],
                             start=True, stop=True).then_inc(*inc("mm", 1))
                mm_c = C["mm"]

                @block.vector
                def _(v, tc=tc, mm_c=mm_c, fb=fb, bkp=bkp):
                    v.wait_ge(SM["mm"], mm_c)
                    v.tensor_add(f32buf[:, fb + tc * 512:fb + (tc + 1) * 512],
                                 f32buf[:, fb + tc * 512:fb + (tc + 1) * 512],
                                 bkp[tc % 2][:, :])
                    v.tensor_mul(scr[:, 6144:6656],
                                 f32buf[:, fb + tc * 512:fb + (tc + 1) * 512],
                                 f32buf[:, fb + tc * 512:fb + (tc + 1) * 512])\
                        .then_inc(*inc("ve", 1))
                evacVe[(ph, tc % 2)] = C["ve"]
                ve_sq = C["ve"]

                @block.tensor
                def _(t, h=h, tc=tc, ve_sq=ve_sq):
                    t.wait_ge(SM["ve"], ve_sq)
                    t.matmul(Bp[1][0:4, :],
                             smal[:, 272 + tc * 4:272 + (tc + 1) * 4],
                             scr[:, 6144:6656],
                             start=(h == 0 and tc == 0),
                             stop=(h == 31 and tc == 3))\
                        .then_inc(*inc("mm", 1))
            ve_corr = C["ve"]

            @block.gpsimd
            def _(g, h=h, fb=fb, ve_corr=ve_corr):
                g.wait_ge(SM["ve"], ve_corr)
                g.dma_start(out=corr_dram[h], in_=f32buf[:, fb:fb + T])\
                    .then_inc(*inc("gs"))
            corrGs[h] = C["gs"]
            corrMM[h] = C["mm"]
            if h < 8:
                emit_t2_it(h)

        emit_t2_tail()
        mm_rms = C["mm"]

        @block.vector
        def _(v):
            v.wait_ge(SM["mm"], mm_rms)
            v.tensor_copy(ltile[0:4, 0:512], Bp[1][0:4, :])\
                .then_inc(*inc("ve", 1))
        ve_rf = C["ve"]

        @block.sync
        def _(s):
            s.wait_ge(SM["ve"], ve_rf)
            s.dma_start(out=rms_dram, in_=ltile[0:4, 0:512])\
                .then_inc(*inc("xs"))
        xs_rf = C["xs"]

        @block.sync
        def _(s):
            s.wait_ge(SM["xs"], xs_rf)
            s.dma_start(out=rowbuf[0:1, 0:T],
                        in_=rms_dram.rearrange("(x a) b -> x (a b)", x=1))\
                .then_inc(*inc("al"))
        al_rf = C["al"]

        @block.vector
        def _(v):
            v.wait_ge(SM["al"], al_rf)
            v.tensor_scalar_mul(rowbuf[0:1, T:2 * T], rowbuf[0:1, 0:T],
                                1.0 / D).then_inc(*inc("ve", 1))
        ve_ms = C["ve"]

        @block.scalar
        def _(sc):
            sc.wait_ge(SM["ve"], ve_ms)
            sc.activation(rowbuf[0:1, 2 * T:3 * T], rowbuf[0:1, T:2 * T],
                          AF.Sqrt, bias=bias_sb[0:1, 81:82])\
                .then_inc(*inc("ac", 1))
        ac_r = C["ac"]

        @block.vector
        def _(v):
            v.wait_ge(SM["ac"], ac_r)
            v.reciprocal(rowbuf[0:1, T:2 * T], rowbuf[0:1, 2 * T:3 * T])\
                .then_inc(*inc("ve", 1))
        ve_rr = C["ve"]

        for tc in range(4):
            @block.tensor
            def _(t, tc=tc):
                t.wait_ge(SM["ve"], ve_rr)
                if tc > 0:
                    t.wait_ge(SM["ve"], C["ve"])
                t.matmul(A[0][:, :], onesr[0:1, :],
                         rowbuf[0:1, T + tc * 512:T + (tc + 1) * 512],
                         start=True, stop=True).then_inc(*inc("mm", 1))
            mm_r2 = C["mm"]

            @block.vector
            def _(v, tc=tc, mm_r2=mm_r2):
                v.wait_ge(SM["mm"], mm_r2)
                v.tensor_copy(R2[:, tc * 512:(tc + 1) * 512], A[0][:, :])\
                    .then_inc(*inc("ve", 1))

        def run_out(hf):
            gnldF = [None, None]
            gtldF = [None, None]
            gnVe = {}
            for h in range(H1):
                p = h % 2

                if p == 0:
                    @block.sync
                    def _(s, h=h, p=p):
                        if gnldF[p] is None:
                            s.wait_ge(SM["ve"], C["ve"])
                            s.wait_ge(SM["xs"], C["xs"])
                            s.wait_ge(SM["gs"], C["gs"])
                        else:
                            s.wait_ge(SM["ve"], gnldF[p])
                        s.dma_start(out=f32a[:, p * TH:(p + 1) * TH],
                                    in_=corr_dram[h][:, hf * TH:
                                                     (hf + 1) * TH])\
                            .then_inc(*inc("al"))
                    al_gn = ("al", C["al"])
                else:
                    @block.scalar
                    def _(sc, h=h, p=p):
                        if gnldF[p] is None:
                            sc.wait_ge(SM["ve"], C["ve"])
                            sc.wait_ge(SM["gs"], C["gs"])
                        else:
                            sc.wait_ge(SM["ve"], gnldF[p])
                        sc.dma_start(out=f32a[:, p * TH:(p + 1) * TH],
                                     in_=corr_dram[h][:, hf * TH:
                                                      (hf + 1) * TH])\
                            .then_inc(*inc("xb"))
                    al_gn = ("xb", C["xb"])

                @block.gpsimd
                def _(g, h=h, p=p):
                    if gtldF[p] is None:
                        g.wait_ge(SM["ve"], C["ve"])
                    else:
                        g.wait_ge(SM["ve"], gtldF[p])
                    g.dma_start(out=gatef[:, 0:TH] if p == 0
                                else f32buf[:, 3072:4096],
                                in_=gate_out[h][:, hf * TH:(hf + 1) * TH])\
                        .then_inc(*inc("wl"))
                wl_gn = C["wl"]

                @block.vector
                def _(v, h=h, p=p, al_gn=al_gn, wl_gn=wl_gn):
                    v.wait_ge(SM[al_gn[0]], al_gn[1])
                    v.wait_ge(SM["wl"], wl_gn)
                    gbuf = gatef[:, 0:TH] if p == 0 else f32buf[:, 3072:4096]
                    v.tensor_single_scalar(f32a[:, p * TH:(p + 1) * TH],
                                           f32a[:, p * TH:(p + 1) * TH],
                                           nw_sb[:, h:h + 1], ALU.mult)
                    v.tensor_mul(f32a[:, p * TH:(p + 1) * TH],
                                 f32a[:, p * TH:(p + 1) * TH],
                                 R2[:, hf * TH:(hf + 1) * TH])
                    v.tensor_mul(gnT[:, h * TH:(h + 1) * TH],
                                 f32a[:, p * TH:(p + 1) * TH], gbuf)\
                        .then_inc(*inc("ve", 1))
                gnldF[p] = C["ve"]
                gtldF[p] = C["ve"]
                gnVe[h] = C["ve"]

            slabF = [None, None]
            bankF = [None, None]
            rldF = [None, None]
            ostgF = [None, None]
            for o in range(32):
                p = o % 2
                wb = p * 4096

                if hf == 0 and o < 2:
                    wl_o = pre_wl[o]
                else:
                    @block.gpsimd
                    def _(g, o=o, p=p, wb=wb):
                        if slabF[p] is None:
                            g.wait_ge(SM["mm"], C["mm"])
                        else:
                            g.wait_ge(SM["mm"], slabF[p])
                        g.dma_start(
                            out=wslab[:, wb:wb + 4096]
                            .rearrange("p (c m) -> p c m", c=32),
                            in_=outw_slabs[o]
                            .rearrange("(c p) m -> p c m", p=128))\
                            .then_inc(*inc("wl"))
                    wl_o = C["wl"]

                @block.scalar
                def _(sc, o=o, p=p):
                    if rldF[p] is None:
                        sc.wait_ge(SM["ve"], C["ve"])
                    else:
                        sc.wait_ge(SM["ve"], rldF[p])
                    sc.dma_start(out=f32buf[:, 2048 + p * TH:
                                            2048 + (p + 1) * TH],
                                 in_=r_out[o][:, hf * TH:(hf + 1) * TH])\
                        .then_inc(*inc("xb"))
                al_r = C["xb"]

                @block.tensor
                def _(t, o=o, p=p, wb=wb, wl_o=wl_o):
                    t.wait_ge(SM["wl"], wl_o)
                    if bankF[p] is not None:
                        t.wait_ge(SM["ve"], bankF[p])
                    bk = (A[0], A[1]) if p == 0 else (A[2], A[3])
                    for d in range(32):
                        if o < 6:
                            t.wait_ge(SM["ve"], gnVe[d])
                        for tc in range(2):
                            t.matmul(bk[tc][:, :],
                                     wslab[:, wb + d * 128:wb + (d + 1) * 128],
                                     gnT[:, d * TH + tc * 512:
                                         d * TH + (tc + 1) * 512],
                                     start=(d == 0), stop=False)
                    for tc in range(2):
                        t.matmul(bk[tc][:, :],
                                 wslab[0:32, 8192 + o * 128:8192 + (o + 1) * 128],
                                 scr[0:32, 7168 + hf * TH + tc * 512:
                                     7168 + hf * TH + (tc + 1) * 512],
                                 start=False, stop=True).then_inc(*inc("mm", 1))
                mm_o = C["mm"]
                slabF[p] = mm_o

                @block.vector
                def _(v, o=o, p=p, mm_o=mm_o, al_r=al_r):
                    v.wait_ge(SM["mm"], mm_o)
                    v.wait_ge(SM["xb"], al_r)
                    if ostgF[p] is not None:
                        v.wait_ge(SM["xs"], ostgF[p])
                    bk = (A[0], A[1]) if p == 0 else (A[2], A[3])
                    v.tensor_add(f32a[:, p * TH:p * TH + 512],
                                 f32buf[:, 2048 + p * TH:2048 + p * TH + 512],
                                 bk[0][:, :])
                    v.tensor_add(f32a[:, p * TH + 512:p * TH + 1024],
                                 f32buf[:, 2048 + p * TH + 512:
                                        2048 + p * TH + 1024],
                                 bk[1][:, :]).then_inc(*inc("ve", 1))
                ve_o = C["ve"]
                bankF[p] = ve_o
                rldF[p] = ve_o

                @block.sync
                def _(s, o=o, p=p, ve_o=ve_o):
                    s.wait_ge(SM["ve"], ve_o)
                    s.dma_start(out=out2[o * 128:(o + 1) * 128,
                                         hf * TH:(hf + 1) * TH],
                                in_=f32a[:, p * TH:(p + 1) * TH])\
                        .then_inc(*inc("xs"))
                ostgF[p] = C["xs"]

        run_out(0)
        run_out(1)

        @block.sync
        def _(s):
            s.wait_ge(SM["xs"], C["xs"])

        @block.gpsimd
        def _(g):
            g.wait_ge(SM["xs"], C["xs"])

    return nc



_NC = None


def kernel(**inputs):
    global _NC
    h = host_prep(inputs)
    if _NC is None:
        _NC = build_fused()
    nc = _NC

    shared = {k: h[k] for k in
              ["wq_slabs", "bq_sel", "Mt", "z8", "l0out_slabs", "b_base",
               "a_allT", "wgT", "qkv_slabs", "gw1_slabs",
               "b_l1", "a_l1", "limesT", "diagT", "qd_tab", "qdbd_tab",
               "kdec", "ones_bf", "ones_f32r", "ident_bf", "ident_f", "e16",
               "cst", "limes", "bmat", "norm_w", "outw_slabs", "sel4"]}
    in_maps = [dict(shared, x_tm=h["x_tm"][c], x_fm=h["x_fm"][c],
                    coefb4=h["coefb"][c])
               for c in range(NCORES)]
    res = run_bass_kernel_spmd(nc, in_maps, core_ids=list(range(NCORES)))
    outs = [np.ascontiguousarray(res.results[c]["out2"].T)
            for c in range(NCORES)]
    return np.concatenate(outs, axis=0).reshape(2, 8192, D)


# version 10
# speedup vs baseline: 1.0010x; 1.0010x over previous
import numpy as np
import ml_dtypes
from contextlib import ExitStack

import concourse.bass as bass
import concourse.mybir as mybir
from concourse.bass_utils import run_bass_kernel_spmd

F32 = mybir.dt.float32
BF16 = mybir.dt.bfloat16
AF = mybir.ActivationFunctionType
ALU = mybir.AluOpType
AX = mybir.AxisListType.X
BF = ml_dtypes.bfloat16

NCORES = 8
D = 4096
T = 2048
TH = 1024
NCD = 32
L0_HSEL = [0, 4, 8, 12, 16, 20, 24, 28]
L0_EPS = 1e-6
LN_EPS = 1e-5
H1, DH, BLK = 32, 128, 256
LAYER_IDX, N_LAYERS = 1, 32
RMS_EPS = 1e-5
BIG = 1.0e30

DEBUG_OUT = ()


def _slopes():
    base = 1.0 / 2.0 ** (8.0 / H1)
    expo = np.arange(H1, dtype=np.float64) + 1.0
    factor = 1.0 - LAYER_IDX / (N_LAYERS - 1 + 1e-05) + 1e-05
    return base ** expo * factor


BD_TAB = [float(v) for v in np.exp(-_slopes() * BLK)]


def _f32(a):
    return np.ascontiguousarray(np.asarray(a), dtype=np.float32)


def _bf(a):
    return np.ascontiguousarray(np.asarray(a, dtype=np.float32).astype(BF))


def _slabs(WT):
    d_in, d_out = WT.shape
    return np.ascontiguousarray(WT.reshape(d_in, d_out // 128, 128).transpose(1, 0, 2))


def host_prep(inputs):
    h = {}
    x = np.asarray(inputs["L0_attn_hidden_states"]).reshape(NCORES * T, D)
    h["x_tm"] = [_f32(x[c * T:(c + 1) * T]) for c in range(NCORES)]
    h["x_fm"] = [_f32(x[c * T:(c + 1) * T].T) for c in range(NCORES)]

    Wq = np.asarray(inputs["L0_attn_Wq"])
    bq = np.asarray(inputs["L0_attn_bq"])
    rows = np.concatenate([np.arange(hh * 128, (hh + 1) * 128) for hh in L0_HSEL])
    h["wq_slabs"] = _f32(_slabs(Wq[rows].T))
    h["bq_sel"] = _f32(bq[rows].reshape(8, 128).T)
    h["Mt"] = _f32(inputs["L0_attn_M"])
    h["z8"] = _f32(np.asarray(inputs["L0_attn_z"]).T)

    Wo = np.asarray(inputs["L0_attn_Wo"])
    Wb = np.asarray(inputs["L0_ffn_W_base"])
    B0 = np.asarray(inputs["L0_ffn_B"])
    B_allM = B0.transpose(0, 2, 1).reshape(128, 4096)
    h["l0out_slabs"] = _f32(np.concatenate(
        [_slabs(Wo.T), _slabs(Wb.T), _slabs(B_allM)], axis=1))
    h["b_base"] = _f32(np.asarray(inputs["L0_ffn_b_base"]).reshape(32, 128).T)
    h["a_allT"] = _f32(np.asarray(inputs["L0_ffn_A"]).reshape(128, D).T)
    h["wgT"] = _f32(np.asarray(inputs["L0_ffn_W_gate"]).T)

    qkvT = np.asarray(inputs["L1_attn_qkv_w"]).T
    h["qkv_slabs"] = _f32(qkvT.reshape(D, 32, 3, 128).transpose(1, 2, 0, 3))
    h["gw1_slabs"] = _f32(np.concatenate(
        [_slabs(np.asarray(inputs["L1_attn_gate_w"]).T),
         _slabs(np.asarray(inputs["L1_ffn_W"]).T)], axis=1))
    h["outw_slabs"] = _f32(_slabs(np.asarray(inputs["L1_attn_out_w"]).T))
    h["b_l1"] = _f32(np.asarray(inputs["L1_ffn_b"]).reshape(32, 128).T)
    h["norm_w"] = _f32(np.asarray(inputs["L1_attn_norm_w"]).reshape(32, 128).T)
    h["a_l1"] = _f32(inputs["L1_ffn_A"])
    h["limesT"] = _f32(np.asarray(inputs["L1_ffn_LiMEs"]).T)
    h["limes"] = _f32(inputs["L1_ffn_LiMEs"])
    h["bmat"] = _f32(inputs["L1_ffn_Bmat"])

    sl = _slopes()
    r = np.arange(BLK, dtype=np.float64)
    j, i = r[:, None], r[None, :]
    diag = np.where(i - j >= 0, np.exp(-sl[:, None, None] * (i - j)), 0.0)
    h["diagT"] = _f32(diag.reshape(H1, 2, 128, BLK))
    qd = np.exp(-sl[:, None] * (r[None, :] + 1.0))
    h["qd_tab"] = _bf(np.broadcast_to(qd[:, None, :], (H1, 128, BLK)))
    bd = np.exp(-sl * BLK)
    ii = np.arange(T, dtype=np.float64)
    qdbd = np.exp(-sl[:, None] * ((ii % BLK) + 1.0)[None, :]) * \
        (bd[:, None] ** (ii // BLK)[None, :])
    h["qdbd_tab"] = _bf(np.broadcast_to(qdbd[:, None, :], (H1, 128, T)))
    kd = np.exp(-sl[:, None] * (255.0 - r)[None, :])
    h["kdec"] = _f32(kd.reshape(H1, 2, 128).transpose(2, 0, 1).reshape(128, 64))

    h["ones_bf"] = _bf(np.ones((128, 1)))
    h["ones_f32r"] = _f32(np.ones((1, 128)))
    h["ident_bf"] = _bf(np.eye(128))
    h["ident_f"] = _f32(np.eye(128))
    e16 = np.zeros((8, 128), dtype=np.float32)
    for e in range(8):
        e16[e, e * 16:(e + 1) * 16] = 1.0
    h["e16"] = _bf(e16)
    s4 = np.zeros((128, 16), dtype=np.float32)
    for j in range(4):
        s4[:, j * 4 + j] = 1.0
    h["sel4"] = _bf(s4)
    cst = np.zeros((128, 4), dtype=np.float32)
    cst[:, 0] = LN_EPS
    cst[:, 1] = RMS_EPS
    cst[:, 2] = BIG
    cst[:, 3] = 1.0
    h["cst"] = cst
    coefb = []
    for c in range(NCORES):
        k = c % 4
        cb = np.zeros((4, 128, H1), dtype=np.float32)
        for s in range(4):
            if s < k:
                for hh in range(H1):
                    cb[s, :, hh] = bd[hh] ** (8 * (k - 1 - s))
        coefb.append(cb)
    h["coefb"] = coefb
    return h


def build_fused():
    nc = bass.Bass("TRN2", target_bir_lowering=False)

    def din(name, shape, dt=F32):
        return nc.dram_tensor(name, shape, dt, kind="ExternalInput").ap()

    def dout(name, shape, dt=F32):
        return nc.dram_tensor(name, shape, dt, kind="ExternalOutput").ap()

    def dint(name, shape, dt=F32):
        kind = "ExternalOutput" if name in DEBUG_OUT else "Internal"
        return nc.dram_tensor(name, shape, dt, kind=kind).ap()

    x_tm = din("x_tm", [T, D])
    x_fm = din("x_fm", [D, T])
    wq_slabs = din("wq_slabs", [8, D, 128])
    bq_sel = din("bq_sel", [128, 8])
    Mt = din("Mt", [8, 128, 128])
    z8 = din("z8", [128, 8])
    l0out_slabs = din("l0out_slabs", [32, 8320, 128])
    b_base = din("b_base", [128, 32])
    a_allT = din("a_allT", [D, 128])
    wgT = din("wgT", [D, 8])
    qkv_slabs = din("qkv_slabs", [32, 3, D, 128])
    gw1_slabs = din("gw1_slabs", [32, 2 * D, 128])
    b_l1 = din("b_l1", [128, 32])
    a_l1 = din("a_l1", [D, 32])
    limesT = din("limesT", [32, 8])
    diagT = din("diagT", [H1, 2, 128, BLK])
    qd_tab = din("qd_tab", [H1, 128, BLK], BF16)
    qdbd_tab = din("qdbd_tab", [H1, 128, T], BF16)
    kdec = din("kdec", [128, 64])
    ones_bf = din("ones_bf", [128, 1], BF16)
    ones_f32r = din("ones_f32r", [1, 128])
    ident_bf = din("ident_bf", [128, 128], BF16)
    ident_f = din("ident_f", [128, 128])
    e16 = din("e16", [8, 128], BF16)
    cst = din("cst", [128, 4])
    limes = din("limes", [8, 32])
    bmat = din("bmat", [32, 4096])
    norm_w = din("norm_w", [128, 32])
    outw_slabs = din("outw_slabs", [32, D, 128])
    coefb4 = din("coefb4", [4, 128, H1])
    sel4 = din("sel4", [128, 16], BF16)

    xn_dram = dint("xn_dram", [T, D], BF16)
    x1_dram = dint("x1_dram", [NCD, 128, T])
    mxb = dint("mxb", [16, 1])

    r_out = dint("r_out", [NCD, 128, T])
    gate_out = dint("gate_out", [NCD, 128, T], BF16)
    attnp_out = dint("attnp_out", [H1, 128, T])
    qdbd_out = dint("qdbd_out", [H1, 128, T], BF16)
    s_src = dint("s_src", [H1 * 128, 128])
    sall1 = dint("sall1", [4 * 16 * 128, 128])
    sall2 = dint("sall2", [4 * 16 * 128, 128])
    mx_src = dint("mx_src", [2, 1])
    mxg = dint("mxg", [16, 1])
    corr_dram = dint("corr_dram", [H1, 128, T])
    rms_dram = dint("rms_dram", [4, 512])
    st_dram = dint("st_dram", [2, 2, 1024])
    out2 = dout("out2", [D, T])

    actT = nc.alloc_sbuf_tensor("actT", [128, NCD * TH], BF16)
    scr = nc.alloc_sbuf_tensor("scr", [128, 9 * TH], BF16)
    wslab = nc.alloc_sbuf_tensor("wslab", [128, 2 * 8320], BF16)
    f32buf = nc.alloc_sbuf_tensor("f32buf", [128, 4096], F32)
    bcast = nc.alloc_sbuf_tensor("bcast", [128, 2 * TH], F32)
    attnp_sb = nc.alloc_sbuf_tensor("attnp_sb", [128, TH], F32)
    diag_sb = nc.alloc_sbuf_tensor("diag_sb", [128, 1024], F32)
    qd_sb = nc.alloc_sbuf_tensor("qd_sb", [128, 2 * BLK], BF16)
    qdbd_sb = nc.alloc_sbuf_tensor("qdbd_sb", [128, TH], BF16)
    kv_sb = nc.alloc_sbuf_tensor("kv_sb", [128, 256], F32)
    kvb_sb = nc.alloc_sbuf_tensor("kvb_sb", [128, 256], BF16)
    atw = nc.alloc_sbuf_tensor("atw", [128, 1536], BF16)
    ltile = nc.alloc_sbuf_tensor("ltile", [128, 512], F32)
    rowbuf = nc.alloc_sbuf_tensor("rowbuf", [1, 3 * T], F32)
    colbuf = nc.alloc_sbuf_tensor("colbuf", [128, 96], F32)
    bias_sb = nc.alloc_sbuf_tensor("bias_sb", [128, 112], F32)
    mz_sb = nc.alloc_sbuf_tensor("mz_sb", [128, 8 * 128 + 8], BF16)
    smal = nc.alloc_sbuf_tensor("smal", [128, 512], BF16)
    onesr = nc.alloc_sbuf_tensor("onesr", [1, 128], F32)
    idf = nc.alloc_sbuf_tensor("idf", [128, 128], F32)
    lgT = nc.alloc_sbuf_tensor("lgT", [8, T], BF16)
    zt_sb = nc.alloc_sbuf_tensor("zt_sb", [32, T], F32)
    hs_sb = nc.alloc_sbuf_tensor("hs_sb", [8, T], F32)
    mrow = nc.alloc_sbuf_tensor("mrow", [1, 32], F32)
    limes_sb = nc.alloc_sbuf_tensor("limes_sb", [8, 32], BF16)
    nw_sb = nc.alloc_sbuf_tensor("nw_sb", [128, 32], F32)
    coefq = nc.alloc_sbuf_tensor("coefq", [128, 4 * H1], F32)

    A = [nc.alloc_psum_tensor(f"psA{i}", [128, 512], F32) for i in range(4)]
    Bp = [nc.alloc_psum_tensor(f"psB{i}", [128, 512], F32) for i in range(2)]
    Cp = nc.alloc_psum_tensor("psC", [128, 1024], BF16)
    Dp = nc.alloc_psum_tensor("psD", [128, 512], F32)

    with ExitStack() as es:
        sn = ["wl", "al", "xb", "mm", "ac", "ve", "gs", "xs", "vf", "af",
              "pf", "c1", "c2", "c3"]
        SM = {n: es.enter_context(nc.semaphore(n)) for n in sn}
        C = {n: 0 for n in sn}
        block = es.enter_context(nc.Block())

        def inc(n, v=16):
            C[n] += v
            return SM[n], v

        def wall(eng, *names):
            for n in names:
                if C[n] > 0:
                    eng.wait_ge(SM[n], C[n])

        def vF(v, ins):
            C["vf"] += 1
            ins.then_inc(SM["vf"], 1)
            v.wait_ge(SM["vf"], C["vf"])

        def aF(sc, ins):
            C["af"] += 1
            ins.then_inc(SM["af"], 1)
            sc.wait_ge(SM["af"], C["af"])

        @block.sync
        def _(s):
            s.dma_start(out=bias_sb[:, 0:8], in_=bq_sel).then_inc(*inc("al"))
            s.dma_start(out=bias_sb[:, 8:40], in_=b_base).then_inc(*inc("al"))
            s.dma_start(out=bias_sb[:, 40:72], in_=b_l1).then_inc(*inc("al"))
            s.dma_start(out=colbuf[:, 0:64], in_=kdec).then_inc(*inc("al"))
            s.dma_start(out=onesr[:, :], in_=ones_f32r).then_inc(*inc("al"))
            s.dma_start(out=idf[:, :], in_=ident_f).then_inc(*inc("al"))
            s.dma_start(out=bias_sb[:, 80:84], in_=cst).then_inc(*inc("al"))
            s.dma_start(out=nw_sb[:, :], in_=norm_w).then_inc(*inc("al"))
            s.dma_start(out=coefq[:, :].rearrange("p (s h) -> p s h", s=4),
                        in_=coefb4.rearrange("s p h -> p s h"))\
                .then_inc(*inc("al"))

        @block.gpsimd
        def _(g):
            g.dma_start(out=mz_sb[:, 0:1024].rearrange("p (h e) -> p h e", h=8),
                        in_=Mt.rearrange("h d e -> d h e")).then_inc(*inc("wl"))
            g.dma_start(out=mz_sb[:, 1024:1032], in_=z8).then_inc(*inc("wl"))
            g.dma_start(out=smal[:, 0:128], in_=ident_bf).then_inc(*inc("wl"))
            g.dma_start(out=smal[:, 128:129], in_=ones_bf).then_inc(*inc("wl"))
            g.dma_start(out=smal[0:8, 129:257], in_=e16).then_inc(*inc("wl"))
            g.dma_start(out=smal[0:32, 257:265], in_=limesT).then_inc(*inc("wl"))
            g.dma_start(out=limes_sb[0:8, :], in_=limes).then_inc(*inc("wl"))
            g.dma_start(out=smal[:, 272:288], in_=sel4).then_inc(*inc("wl"))

        @block.vector
        def _(v):
            wall(v, "al")
            v.tensor_scalar_mul(bias_sb[:, 72:80], bias_sb[:, 0:8], -1.0)\
                .then_inc(*inc("ve", 1))

        al0, ac0, ve0, xs0 = C["al"], C["ac"], C["ve"], C["xs"]
        xa = f32buf[:, 0:4096]
        sqb = bcast[:, 0:2048]

        @block.sync
        def _(s):
            for i in range(16):
                if i >= 1:
                    s.wait_ge(SM["ac"], ac0 + 2 * i)
                    s.wait_ge(SM["ve"], ve0 + 2 * i - 1)
                s.dma_start(out=xa, in_=x_tm[i * 128:(i + 1) * 128, :])\
                    .then_inc(*inc("al"))
                s.wait_ge(SM["ac"], ac0 + 2 * i + 2)
                s.dma_start(out=xn_dram[i * 128:(i + 1) * 128, :],
                            in_=scr[:, 4096:8192]).then_inc(*inc("xs"))

        @block.vector
        def _(v):
            for i in range(16):
                v.wait_ge(SM["al"], al0 + 16 * (i + 1))
                v.reduce_sum(colbuf[:, 64:65], xa, AX)
                v.tensor_mul(sqb, xa[:, 0:2048], xa[:, 0:2048])
                v.reduce_sum(colbuf[:, 66:67], sqb, AX)
                v.tensor_mul(sqb, xa[:, 2048:4096], xa[:, 2048:4096])
                vF(v, v.reduce_sum(colbuf[:, 67:68], sqb, AX))
                vF(v, v.tensor_add(colbuf[:, 65:66], colbuf[:, 66:67],
                                   colbuf[:, 67:68]))
                v.tensor_scalar_mul(colbuf[:, 68:69], colbuf[:, 64:65], 1.0 / D)
                vF(v, v.tensor_scalar_mul(colbuf[:, 69:70], colbuf[:, 65:66],
                                          1.0 / D))
                vF(v, v.tensor_mul(colbuf[:, 70:71], colbuf[:, 68:69],
                                   colbuf[:, 68:69]))
                v.tensor_sub(colbuf[:, 71:72], colbuf[:, 69:70], colbuf[:, 70:71])\
                    .then_inc(*inc("ve", 1))
                v.wait_ge(SM["ac"], ac0 + 2 * i + 1)
                vF(v, v.reciprocal(colbuf[:, 73:74], colbuf[:, 72:73]))
                vF(v, v.tensor_mul(colbuf[:, 74:75], colbuf[:, 68:69],
                                   colbuf[:, 73:74]))
                v.tensor_scalar_mul(colbuf[:, 75:76], colbuf[:, 74:75], -1.0)\
                    .then_inc(*inc("ve", 1))

        @block.scalar
        def _(sc):
            for i in range(16):
                sc.wait_ge(SM["ve"], ve0 + 2 * i + 1)
                sc.activation(colbuf[:, 72:73], colbuf[:, 71:72], AF.Sqrt,
                              bias=bias_sb[:, 80:81]).then_inc(*inc("ac", 1))
                sc.wait_ge(SM["ve"], ve0 + 2 * i + 2)
                if i >= 1:
                    sc.wait_ge(SM["xs"], xs0 + 16 * i)
                sc.activation(scr[:, 4096:8192], xa, AF.Identity,
                              bias=colbuf[:, 75:76], scale=colbuf[:, 73:74])\
                    .then_inc(*inc("ac", 1))

        def load_xnT():
            @block.sync
            def _(s):
                wall(s, "xs", "ve", "mm")

            @block.sync
            def _(s):
                pass
            return None

        def chain(nd, lhs_tile, rhs_tile, evac, banks=(0, 1), wait_before=None,
                  wait_at=None):
            stops = []

            @block.tensor
            def _(t):
                if wait_before:
                    wait_before(t)
                for d in range(nd):
                    if wait_at and d in wait_at:
                        t.wait_ge(SM["ve"], wait_at[d])
                    for tc in range(2):
                        ins = t.matmul(A[banks[tc]][:, :], lhs_tile(d),
                                       rhs_tile(d, tc), start=(d == 0),
                                       stop=(d == nd - 1))
                        if d == nd - 1:
                            ins.then_inc(*inc("mm", 1))
                            stops.append(C["mm"])
            for tc in range(2):
                evac(tc, A[banks[tc]], stops[tc])
            return stops

        def xnT_fill(hf):
            @block.sync
            def _(s):
                if hf == 0:
                    s.wait_ge(SM["xs"], xs0 + 128)
                else:
                    wall(s, "xs", "ve", "mm")
                for c in range(NCD):
                    s.dma_start(out=actT[:, c * TH:(c + 1) * TH],
                                in_=xn_dram[hf * TH:(hf + 1) * TH,
                                            c * 128:(c + 1) * 128],
                                transpose=True).then_inc(*inc("xb"))

        def run_half_l0(hf):
            xnT_fill(hf)
            xb_need = C["xb"]

            for h in range(8):
                wbase = (h % 2) * 8320

                @block.gpsimd
                def _(g, h=h, wbase=wbase):
                    g.wait_ge(SM["mm"], C["mm"])
                    g.dma_start(
                        out=wslab[:, wbase:wbase + 4096]
                        .rearrange("p (c m) -> p c m", c=32),
                        in_=wq_slabs[h].rearrange("(c p) m -> p c m", p=128))\
                        .then_inc(*inc("wl"))
                wl_need = C["wl"]
                sig = scr[:, 8 * TH:9 * TH]

                def lhs(d, wbase=wbase):
                    return wslab[:, wbase + d * 128:wbase + (d + 1) * 128]

                def rhs(d, tc):
                    return actT[:, d * TH + tc * 512:d * TH + (tc + 1) * 512]

                def evac_sig(tc, ps, mmv, h=h, sig=sig):
                    @block.scalar
                    def _(sc, tc=tc, mmv=mmv, h=h):
                        sc.wait_ge(SM["mm"], mmv)
                        sc.wait_ge(SM["ve"], C["ve"])
                        sc.activation(ltile[:, 0:512], ps[:, :], AF.Relu,
                                      bias=bias_sb[:, h:h + 1])
                        sc.activation(diag_sb[:, 0:512], ps[:, :], AF.Relu,
                                      bias=bias_sb[:, 72 + h:73 + h], scale=-1.0)\
                            .then_inc(*inc("ac", 1))
                        sc.activation(attnp_sb[:, 0:512], diag_sb[:, 0:512],
                                      AF.Exp, scale=-1.0).then_inc(*inc("ac", 1))
                    acv = C["ac"]

                    @block.vector
                    def _(v, tc=tc, acv=acv, sig=sig):
                        v.wait_ge(SM["ac"], acv)
                        v.tensor_add(sig[:, tc * 512:(tc + 1) * 512],
                                     ltile[:, 0:512], attnp_sb[:, 0:512])\
                            .then_inc(*inc("ve", 1))

                def wait_b(t, wl_need=wl_need):
                    t.wait_ge(SM["wl"], wl_need)
                    t.wait_ge(SM["xb"], xb_need)
                    if C["ac"] > 0:
                        t.wait_ge(SM["ac"], C["ac"])
                    if C["ve"] > 0:
                        t.wait_ge(SM["ve"], C["ve"])

                chain(32, lhs, rhs, evac_sig, banks=(0, 1), wait_before=wait_b)

                for tc in range(2):
                    ve_sig = C["ve"]

                    @block.tensor
                    def _(t, h=h, tc=tc, ve_sig=ve_sig, sig=sig):
                        t.wait_ge(SM["ve"], ve_sig)
                        t.matmul(Bp[tc][:, :], mz_sb[:, h * 128:(h + 1) * 128],
                                 sig[:, tc * 512:(tc + 1) * 512],
                                 start=True, stop=True).then_inc(*inc("mm", 1))
                        t.matmul(Dp[0:1, :], mz_sb[:, 1024 + h:1025 + h],
                                 sig[:, tc * 512:(tc + 1) * 512],
                                 start=True, stop=True).then_inc(*inc("mm", 1))
                    mm_nd = C["mm"]

                    @block.vector
                    def _(v, mm_nd=mm_nd):
                        v.wait_ge(SM["mm"], mm_nd)
                        vF(v, v.tensor_copy(rowbuf[0:1, 2 * T:2 * T + 512],
                                            Dp[0:1, :]))
                        vF(v, v.tensor_scalar_add(
                            rowbuf[0:1, 2 * T + 512:2 * T + 1024],
                            rowbuf[0:1, 2 * T:2 * T + 512], L0_EPS))
                        v.reciprocal(rowbuf[0:1, 2 * T:2 * T + 512],
                                     rowbuf[0:1, 2 * T + 512:2 * T + 1024])\
                            .then_inc(*inc("ve", 1))
                    ve_den = C["ve"]

                    @block.tensor
                    def _(t, ve_den=ve_den):
                        t.wait_ge(SM["ve"], ve_den)
                        t.matmul(A[2][:, :], onesr[0:1, :],
                                 rowbuf[0:1, 2 * T:2 * T + 512],
                                 start=True, stop=True).then_inc(*inc("mm", 1))
                    mm_bc = C["mm"]

                    @block.vector
                    def _(v, h=h, tc=tc, mm_bc=mm_bc):
                        v.wait_ge(SM["mm"], mm_bc)
                        if h == 4 and tc == 0:
                            v.wait_ge(SM["xs"], xs0 + 256)
                        v.tensor_copy(attnp_sb[:, 512:1024], Bp[tc][:, :])
                        v.tensor_mul(
                            scr[:, h * TH + tc * 512:h * TH + (tc + 1) * 512],
                            attnp_sb[:, 512:1024], A[2][:, :])\
                            .then_inc(*inc("ve", 1))

            ve_preS3 = C["ve"]
            ac_preS3 = C["ac"]

            @block.gpsimd
            def _(g):
                g.wait_ge(SM["mm"], C["mm"])
                g.dma_start(out=wslab[:, 0:4096]
                            .rearrange("p (c m) -> p c m", c=32),
                            in_=a_allT.rearrange("(c p) m -> p c m", p=128))\
                    .then_inc(*inc("wl"))
                g.dma_start(out=wslab[:, 4096:4352]
                            .rearrange("p (c m) -> p c m", c=32),
                            in_=wgT.rearrange("(c p) m -> p c m", p=128))\
                    .then_inc(*inc("wl"))
            wl_s3 = C["wl"]
            gbuf = scr[:, 8 * TH:9 * TH]

            def lhs_g(d):
                return wslab[:, d * 128:(d + 1) * 128]

            def rhs_x(d, tc):
                return actT[:, d * TH + tc * 512:d * TH + (tc + 1) * 512]

            def evac_g(tc, ps, mmv):
                @block.vector
                def _(v, tc=tc, mmv=mmv):
                    v.wait_ge(SM["mm"], mmv)
                    v.tensor_copy(gbuf[:, tc * 512:(tc + 1) * 512], ps[:, :])\
                        .then_inc(*inc("ve", 1))

            def wait_g(t):
                t.wait_ge(SM["wl"], wl_s3)
                t.wait_ge(SM["ve"], C["ve"])

            chain(32, lhs_g, rhs_x, evac_g, banks=(0, 1), wait_before=wait_g)
            ve_s3g = C["ve"]

            for tc in range(2):
                @block.tensor
                def _(t, tc=tc):
                    t.wait_ge(SM["ve"], C["ve"])
                    for d in range(32):
                        ins = t.matmul(Bp[0][0:8, :],
                                       wslab[:, 4096 + d * 8:4096 + (d + 1) * 8],
                                       rhs_x(d, tc), start=(d == 0),
                                       stop=(d == 31))
                    ins.then_inc(*inc("mm", 1))
                mm_l = C["mm"]

                @block.vector
                def _(v, tc=tc, mm_l=mm_l):
                    v.wait_ge(SM["mm"], mm_l)
                    v.tensor_copy(
                        lgT[0:8, hf * TH + tc * 512:hf * TH + (tc + 1) * 512],
                        Bp[0][0:8, :]).then_inc(*inc("ve", 1))

            rwT = atw[0:8, 256:1280]
            for it in range(8):
                col = hf * TH + it * 128
                ve_l = C["ve"]

                @block.tensor
                def _(t, col=col, ve_l=ve_l):
                    t.wait_ge(SM["ve"], ve_l)
                    t.transpose(Cp[:, 0:8], lgT[0:8, col:col + 128],
                                smal[0:8, 0:8]).then_inc(*inc("mm", 1))
                mm_t = C["mm"]

                @block.vector
                def _(v, mm_t=mm_t):
                    v.wait_ge(SM["mm"], mm_t)
                    vF(v, v.tensor_copy(ltile[:, 0:8], Cp[:, 0:8]))
                    vF(v, v.reduce_max(colbuf[:, 76:77], ltile[:, 0:8], AX))
                    v.tensor_scalar_mul(colbuf[:, 77:78], colbuf[:, 76:77], -1.0)\
                        .then_inc(*inc("ve", 1))
                ve_m = C["ve"]

                @block.scalar
                def _(sc, ve_m=ve_m):
                    sc.wait_ge(SM["ve"], ve_m)
                    sc.activation(ltile[:, 8:16], ltile[:, 0:8], AF.Exp,
                                  bias=colbuf[:, 77:78]).then_inc(*inc("ac", 1))
                ac_e = C["ac"]

                @block.vector
                def _(v, ac_e=ac_e):
                    v.wait_ge(SM["ac"], ac_e)
                    vF(v, v.reduce_sum(colbuf[:, 78:79], ltile[:, 8:16], AX))
                    vF(v, v.reciprocal(colbuf[:, 79:80], colbuf[:, 78:79]))
                    vF(v, v.tensor_single_scalar(ltile[:, 16:24], ltile[:, 8:16],
                                                 colbuf[:, 79:80], ALU.mult))
                    vF(v, v.reduce_max(colbuf[:, 80:81], ltile[:, 16:24], AX))
                    v.tensor_single_scalar(ltile[:, 24:32], ltile[:, 16:24],
                                           colbuf[:, 80:81], ALU.subtract)\
                        .then_inc(*inc("ve", 1))
                ve_d = C["ve"]

                @block.scalar
                def _(sc, ve_d=ve_d):
                    sc.wait_ge(SM["ve"], ve_d)
                    aF(sc, sc.activation(ltile[:, 32:40], ltile[:, 24:32],
                                         AF.Sign))
                    sc.activation(ltile[:, 40:48], ltile[:, 32:40], AF.Identity,
                                  bias=bias_sb[:, 82:83], scale=bias_sb[:, 82:83])\
                        .then_inc(*inc("ac", 1))
                ac_s = C["ac"]

                @block.vector
                def _(v, it=it, ac_s=ac_s):
                    v.wait_ge(SM["ac"], ac_s)
                    vF(v, v.tensor_sub(ltile[:, 48:56], ltile[:, 16:24],
                                       ltile[:, 40:48]))
                    vF(v, v.reduce_max(colbuf[:, 81:82], ltile[:, 48:56], AX))
                    vF(v, v.tensor_single_scalar(ltile[:, 56:64], ltile[:, 16:24],
                                                 colbuf[:, 81:82], ALU.is_ge))
                    vF(v, v.tensor_mul(ltile[:, 64:72], ltile[:, 16:24],
                                       ltile[:, 56:64]))
                    vF(v, v.reduce_sum(colbuf[:, 82:83], ltile[:, 64:72], AX))
                    vF(v, v.reciprocal(colbuf[:, 83:84], colbuf[:, 82:83]))
                    vF(v, v.tensor_single_scalar(ltile[:, 72:80], ltile[:, 64:72],
                                                 colbuf[:, 83:84], ALU.mult))
                    v.tensor_copy(atw[:, 0:8], ltile[:, 72:80])\
                        .then_inc(*inc("ve", 1))
                ve_r = C["ve"]

                @block.tensor
                def _(t, ve_r=ve_r):
                    t.wait_ge(SM["ve"], ve_r)
                    t.transpose(Cp[0:8, 0:128], atw[:, 0:8], smal[:, 0:128])\
                        .then_inc(*inc("mm", 1))
                mm_rt = C["mm"]

                @block.vector
                def _(v, it=it, mm_rt=mm_rt):
                    v.wait_ge(SM["mm"], mm_rt)
                    v.tensor_copy(rwT[0:8, it * 128:(it + 1) * 128],
                                  Cp[0:8, 0:128]).then_inc(*inc("ve", 1))

            for tc in range(2):
                ve_rw = C["ve"]

                @block.tensor
                def _(t, tc=tc, ve_rw=ve_rw):
                    t.wait_ge(SM["ve"], ve_rw)
                    t.matmul(Bp[1][:, :], smal[0:8, 129:257],
                             rwT[0:8, tc * 512:(tc + 1) * 512],
                             start=True, stop=True).then_inc(*inc("mm", 1))
                mm_e = C["mm"]

                @block.vector
                def _(v, tc=tc, mm_e=mm_e):
                    v.wait_ge(SM["mm"], mm_e)
                    v.tensor_mul(
                        gbuf[:, tc * 512:(tc + 1) * 512],
                        gbuf[:, tc * 512:(tc + 1) * 512], Bp[1][:, :])\
                        .then_inc(*inc("ve", 1))
            hw = gbuf
            ve_hw = C["ve"]

            slabF = [None, None]
            bankF = [None, None]
            xstrF = [None, None]
            outF = [None, None]
            for o in range(32):
                p = o % 2
                wbase = p * 8320

                @block.gpsimd
                def _(g, o=o, p=p, wbase=wbase):
                    if slabF[p] is None:
                        g.wait_ge(SM["mm"], C["mm"])
                    else:
                        g.wait_ge(SM["mm"], slabF[p])
                    g.dma_start(
                        out=wslab[:, wbase:wbase + 8320]
                        .rearrange("p (c m) -> p c m", c=65),
                        in_=l0out_slabs[o].rearrange("(c p) m -> p c m", p=128))\
                        .then_inc(*inc("wl"))
                wl_o = C["wl"]

                @block.gpsimd
                def _(g, wbase=wbase, wl_o=wl_o):
                    g.wait_ge(SM["wl"], wl_o)
                    wo4 = wslab[:, wbase:wbase + 4096]\
                        .rearrange("p (h g m) -> p h g m", h=8, g=4)
                    g.tensor_add(wo4[:, :, 0, :], wo4[:, :, 0, :],
                                 wo4[:, :, 1, :]).then_inc(*inc("pf", 1))
                    g.tensor_add(wo4[:, :, 2, :], wo4[:, :, 2, :],
                                 wo4[:, :, 3, :]).then_inc(*inc("pf", 1))
                    g.wait_ge(SM["pf"], C["pf"])
                    g.tensor_add(wo4[:, :, 0, :], wo4[:, :, 0, :],
                                 wo4[:, :, 2, :]).then_inc(*inc("gs", 1))
                gs_o = C["gs"]

                @block.sync
                def _(s, o=o, p=p):
                    if xstrF[p] is None:
                        s.wait_ge(SM["ve"], ve_preS3)
                        s.wait_ge(SM["ac"], ac_preS3)
                    else:
                        s.wait_ge(SM["ve"], xstrF[p])
                    s.dma_start(out=f32buf[:, p * TH:(p + 1) * TH],
                                in_=x_fm[o * 128:(o + 1) * 128,
                                         hf * TH:(hf + 1) * TH])\
                        .then_inc(*inc("al"))
                al_o = C["al"]

                def lhs_o(d, wbase=wbase):
                    if d < 8:
                        return wslab[:, wbase + d * 512:wbase + d * 512 + 128]
                    return wslab[:, wbase + 3072 + d * 128:
                                 wbase + 3072 + (d + 1) * 128]

                def rhs_o(d, tc):
                    if d < 8:
                        return scr[:, d * TH + tc * 512:d * TH + (tc + 1) * 512]
                    if d < 40:
                        dd = d - 8
                        return actT[:, dd * TH + tc * 512:dd * TH + (tc + 1) * 512]
                    return hw[:, tc * 512:(tc + 1) * 512]

                def evac_o(tc, ps, mmv, o=o, p=p, al_o=al_o):
                    ob = 2048 + p * TH
                    @block.scalar
                    def _(sc, tc=tc, mmv=mmv, o=o, p=p, ob=ob):
                        sc.wait_ge(SM["mm"], mmv)
                        if outF[p] is not None:
                            sc.wait_ge(SM["xs"], outF[p])
                        else:
                            sc.wait_ge(SM["xs"], C["xs"])
                        sc.activation(
                            f32buf[:, ob + tc * 512:ob + (tc + 1) * 512],
                            ps[:, :], AF.Identity, bias=bias_sb[:, 8 + o:9 + o])\
                            .then_inc(*inc("ac", 1))
                    ac_o = C["ac"]

                    @block.vector
                    def _(v, tc=tc, ac_o=ac_o, al_o=al_o, p=p, ob=ob):
                        v.wait_ge(SM["ac"], ac_o)
                        v.wait_ge(SM["al"], al_o)
                        v.tensor_add(
                            f32buf[:, ob + tc * 512:ob + (tc + 1) * 512],
                            f32buf[:, ob + tc * 512:ob + (tc + 1) * 512],
                            f32buf[:, p * TH + tc * 512:p * TH + (tc + 1) * 512])
                        v.tensor_copy(
                            atw[:, tc * 512:(tc + 1) * 512],
                            f32buf[:, ob + tc * 512:ob + (tc + 1) * 512])
                        v.tensor_mul(
                            qdbd_sb[:, tc * 512:(tc + 1) * 512],
                            f32buf[:, ob + tc * 512:ob + (tc + 1) * 512],
                            f32buf[:, ob + tc * 512:ob + (tc + 1) * 512])\
                            .then_inc(*inc("ve", 1))

                def wait_o(t, wl_o=wl_o, gs_o=gs_o, p=p):
                    t.wait_ge(SM["wl"], wl_o)
                    t.wait_ge(SM["gs"], gs_o)
                    if bankF[p] is None:
                        t.wait_ge(SM["ac"], C["ac"])
                        t.wait_ge(SM["ve"], ve_s3g)
                    else:
                        t.wait_ge(SM["ac"], bankF[p][0])
                        t.wait_ge(SM["ve"], bankF[p][1])

                stops = chain(41, lhs_o, rhs_o, evac_o,
                              banks=(0, 1) if p == 0 else (2, 3),
                              wait_before=wait_o, wait_at={40: ve_hw})
                slabF[p] = stops[-1]
                bankF[p] = (C["ac"], 0)
                xstrF[p] = C["ve"]
                ve_x1 = C["ve"]

                @block.tensor
                def _(t, o=o, ve_x1=ve_x1):
                    t.wait_ge(SM["ve"], ve_x1)
                    for tc in range(2):
                        t.matmul(Dp[0:4, :],
                                 smal[:, 272 + tc * 4:272 + (tc + 1) * 4],
                                 atw[:, tc * 512:(tc + 1) * 512],
                                 start=(o == 0 and tc == 0),
                                 stop=(o == 31 and tc == 1))
                        ins = t.matmul(Bp[0][0:4, :],
                                       smal[:, 272 + tc * 4:272 + (tc + 1) * 4],
                                       qdbd_sb[:, tc * 512:(tc + 1) * 512],
                                       start=(o == 0 and tc == 0),
                                       stop=(o == 31 and tc == 1))
                    ins.then_inc(*inc("mm", 1))

                @block.sync
                def _(s, o=o, p=p, ve_x1=ve_x1):
                    s.wait_ge(SM["ve"], ve_x1)
                    s.dma_start(out=x1_dram[o][:, hf * TH:(hf + 1) * TH],
                                in_=f32buf[:, 2048 + p * TH:2048 + (p + 1) * TH])\
                        .then_inc(*inc("xs"))
                outF[p] = C["xs"]

            st_mm = C["mm"]

            @block.vector
            def _(v, hf=hf, st_mm=st_mm):
                v.wait_ge(SM["mm"], st_mm)
                if hf == 1:
                    v.wait_ge(SM["xs"], stXs[0])
                v.tensor_copy(bcast[0:2, 0:512], Dp[0:2, :])
                v.tensor_copy(bcast[0:2, 512:1024], Bp[0][0:2, :])\
                    .then_inc(*inc("ve", 1))
            ve_fin = C["ve"]

            @block.sync
            def _(s, hf=hf, ve_fin=ve_fin):
                s.wait_ge(SM["ve"], ve_fin)
                s.dma_start(out=st_dram[hf], in_=bcast[0:2, 0:1024])\
                    .then_inc(*inc("xs"))
            stXs[hf] = C["xs"]

        stXs = {}
        run_half_l0(0)
        run_half_l0(1)

        @block.sync
        def _(s):
            s.wait_ge(SM["xs"], C["xs"])
            for hfx in range(2):
                for tc in range(2):
                    s.dma_start(
                        out=rowbuf[0:1, hfx * 1024 + tc * 512:
                                   hfx * 1024 + (tc + 1) * 512],
                        in_=st_dram[hfx][tc:tc + 1, 0:512])\
                        .then_inc(*inc("al"))
                    s.dma_start(
                        out=rowbuf[0:1, T + hfx * 1024 + tc * 512:
                                   T + hfx * 1024 + (tc + 1) * 512],
                        in_=st_dram[hfx][tc:tc + 1, 512:1024])\
                        .then_inc(*inc("al"))

        @block.vector
        def _(v):
            wall(v, "mm")
            wall(v, "al")
            v.tensor_scalar_mul(rowbuf[0:1, 0:T], rowbuf[0:1, 0:T], 1.0 / D)
            v.tensor_scalar_mul(rowbuf[0:1, T:2 * T], rowbuf[0:1, T:2 * T],
                                1.0 / D)
            v.tensor_mul(rowbuf[0:1, 2 * T:3 * T], rowbuf[0:1, 0:T],
                         rowbuf[0:1, 0:T])
            v.tensor_sub(rowbuf[0:1, T:2 * T], rowbuf[0:1, T:2 * T],
                         rowbuf[0:1, 2 * T:3 * T]).then_inc(*inc("ve", 1))
        ve_v = C["ve"]

        @block.scalar
        def _(sc):
            sc.wait_ge(SM["ve"], ve_v)
            sc.activation(rowbuf[0:1, T:2 * T], rowbuf[0:1, T:2 * T],
                          AF.Sqrt, bias=bias_sb[0:1, 80:81]).then_inc(*inc("ac", 1))
        ac_v = C["ac"]

        @block.vector
        def _(v):
            v.wait_ge(SM["ac"], ac_v)
            v.reciprocal(rowbuf[0:1, T:2 * T], rowbuf[0:1, T:2 * T])
            v.tensor_mul(rowbuf[0:1, 2 * T:3 * T], rowbuf[0:1, 0:T],
                         rowbuf[0:1, T:2 * T])
            v.tensor_scalar_mul(rowbuf[0:1, 2 * T:3 * T],
                                rowbuf[0:1, 2 * T:3 * T], -1.0)\
                .then_inc(*inc("ve", 1))
        ve_rs = C["ve"]

        def run_half_l1(hf):
            @block.tensor
            def _(t):
                t.wait_ge(SM["ve"], ve_rs)
                t.wait_ge(SM["ve"], C["ve"])
                for piece in range(2):
                    t.matmul(A[piece][:, :], onesr[0:1, :],
                             rowbuf[0:1, T + hf * TH + piece * 512:
                                    T + hf * TH + (piece + 1) * 512],
                             start=True, stop=True)
                    t.matmul(A[2 + piece][:, :], onesr[0:1, :],
                             rowbuf[0:1, 2 * T + hf * TH + piece * 512:
                                    2 * T + hf * TH + (piece + 1) * 512],
                             start=True, stop=True).then_inc(*inc("mm", 1))
            mm_bc = C["mm"]

            @block.vector
            def _(v, mm_bc=mm_bc):
                v.wait_ge(SM["mm"], mm_bc)
                for piece in range(2):
                    v.tensor_copy(bcast[:, piece * 512:(piece + 1) * 512],
                                  A[piece][:, :])
                v.tensor_copy(bcast[:, TH:TH + 512], A[2][:, :])
                v.tensor_copy(bcast[:, TH + 512:2 * TH], A[3][:, :])\
                    .then_inc(*inc("ve", 1))

            for o in range(32):
                @block.sync
                def _(s, o=o):
                    s.wait_ge(SM["ve"], C["ve"])
                    s.dma_start(out=f32buf[:, 0:TH],
                                in_=x1_dram[o][:, hf * TH:(hf + 1) * TH])\
                        .then_inc(*inc("al"))
                al_n = C["al"]

                @block.vector
                def _(v, o=o, al_n=al_n):
                    v.wait_ge(SM["al"], al_n)
                    v.wait_ge(SM["mm"], C["mm"])
                    v.tensor_mul(f32buf[:, TH:2 * TH], f32buf[:, 0:TH],
                                 bcast[:, 0:TH])
                    v.tensor_add(actT[:, o * TH:(o + 1) * TH],
                                 f32buf[:, TH:2 * TH], bcast[:, TH:2 * TH])\
                        .then_inc(*inc("ve", 1))

            qbankF = None
            kbankF = None
            qdbdldF = None
            slabPartF = [None, None, None]
            wl_part = {}
            tab_rec = {}
            scanEnd = {}
            headPre = {}
            attnpSt = {}
            scaleVe = {}
            stops_rec = {}

            def load_qkv_part(hh, qk):
                @block.gpsimd
                def _(g, hh=hh, qk=qk):
                    if slabPartF[qk] is None:
                        g.wait_ge(SM["mm"], C["mm"])
                    else:
                        g.wait_ge(SM["mm"], slabPartF[qk])
                    g.dma_start(
                        out=wslab[:, qk * 4096:(qk + 1) * 4096]
                        .rearrange("p (c m) -> p c m", c=32),
                        in_=qkv_slabs[hh][qk]
                        .rearrange("(c p) m -> p c m", p=128))\
                        .then_inc(*inc("wl"))
                wl_part[(hh, qk)] = C["wl"]

            def load_tables(hh):
                tp2 = hh % 2
                sv = scanEnd.get(hh - 2)

                @block.sync
                def _(s, hh=hh, tp2=tp2, sv=sv):
                    if sv is not None:
                        s.wait_ge(SM["ve"], sv)
                    else:
                        s.wait_ge(SM["ve"], C["ve"])
                    s.dma_start(out=diag_sb[:, tp2 * 512:(tp2 + 1) * 512]
                                .rearrange("p (c i) -> p c i", c=2),
                                in_=diagT[hh].rearrange("c p i -> p c i"))\
                        .then_inc(*inc("al"))
                    s.dma_start(out=qd_sb[:, tp2 * BLK:(tp2 + 1) * BLK],
                                in_=qd_tab[hh]).then_inc(*inc("al"))
                    if qdbdldF is not None:
                        s.wait_ge(SM["ve"], qdbdldF)
                    s.dma_start(out=qdbd_sb[:, :],
                                in_=qdbd_tab[hh][:, hf * TH:(hf + 1) * TH])\
                        .then_inc(*inc("al"))
                    if hf == 1:
                        s.wait_ge(SM["gs"], C["gs"])
                        s.dma_start(out=kv_sb[:, tp2 * 128:(tp2 + 1) * 128],
                                    in_=s_src[hh * 128:(hh + 1) * 128, :])\
                            .then_inc(*inc("al"))
                tab_rec[hh] = C["al"]

            def slot(hh, which):
                return scr[:, ((hh % 2) * 4 + which) * TH:
                           ((hh % 2) * 4 + which + 1) * TH]

            def attnp_buf(hh):
                return attnp_sb[:, :] if hh % 2 == 0 else f32buf[:, 0:TH]

            def emit_subchain(hh, qk, lo, hi):
                nonlocal qbankF, kbankF
                bks = (0, 1) if qk != 1 else (2, 3)

                @block.tensor
                def _(t, hh=hh, qk=qk, lo=lo, hi=hi, bks=bks):
                    if lo == 0:
                        t.wait_ge(SM["wl"], wl_part[(hh, qk)])
                        bf = [qbankF, kbankF, qbankF][qk]
                        if bf is None:
                            t.wait_ge(SM["ac"], C["ac"])
                            t.wait_ge(SM["ve"], C["ve"])
                        else:
                            t.wait_ge(SM["ac"], bf)
                    for d in range(lo, hi):
                        for tc in range(2):
                            ins = t.matmul(
                                A[bks[tc]][:, :],
                                wslab[:, qk * 4096 + d * 128:
                                      qk * 4096 + (d + 1) * 128],
                                actT[:, d * TH + tc * 512:
                                     d * TH + (tc + 1) * 512],
                                start=(d == 0), stop=(d == 31))
                            if d == 31:
                                ins.then_inc(*inc("mm", 1))
                                stops_rec.setdefault((hh, qk), []).append(
                                    C["mm"])
                if hi == 32:
                    sv = scanEnd.get(hh - 2)
                    for tc in range(2):
                        mmv = stops_rec[(hh, qk)][tc]

                        @block.scalar
                        def _(sc, tc=tc, mmv=mmv, hh=hh, qk=qk, sv=sv):
                            sc.wait_ge(SM["mm"], mmv)
                            if tc == 0:
                                if sv is not None:
                                    sc.wait_ge(SM["ve"], sv)
                                else:
                                    sc.wait_ge(SM["ve"], C["ve"])
                            sc.activation(
                                slot(hh, qk)[:, tc * 512:(tc + 1) * 512],
                                A[bks[tc]][:, :], AF.Silu)\
                                .then_inc(*inc("ac", 1))
                    if qk == 1:
                        kbankF = C["ac"]
                    else:
                        qbankF = C["ac"]
                    slabPartF[qk] = stops_rec[(hh, qk)][-1]

            def emit_head_start(hh):
                nonlocal qdbdldF
                tp2 = hh % 2
                ac_qkv = C["ac"]
                kvr = kv_sb[:, tp2 * 128:(tp2 + 1) * 128]
                kvbr = kvb_sb[:, tp2 * 128:(tp2 + 1) * 128]
                stp = attnpSt.get(hh - 2)

                @block.vector
                def _(v, hh=hh, ac_qkv=ac_qkv, kvr=kvr, kvbr=kvbr, stp=stp):
                    v.wait_ge(SM["ac"], ac_qkv)
                    if stp is not None:
                        v.wait_ge(SM["xs"], stp)
                    v.wait_ge(SM["al"], tab_rec[hh])
                    if hf == 0:
                        v.wait_ge(SM["gs"], C["gs"])
                        v.memset(kvr, 0.0)
                    v.tensor_copy(kvbr, kvr)
                    v.tensor_mul(slot(hh, 3), slot(hh, 0), qdbd_sb[:, :])\
                        .then_inc(*inc("ve", 1))
                headPre[hh] = C["ve"]
                qdbdldF = C["ve"]

                @block.sync
                def _(s, hh=hh):
                    s.wait_ge(SM["ve"], headPre[hh])
                    s.dma_start(out=qdbd_out[hh][:, hf * TH:(hf + 1) * TH],
                                in_=slot(hh, 3)).then_inc(*inc("xs"))

            scr8 = 8 * TH

            def atw_reg(part):
                if part % 2 == 0:
                    return (atw[:, 0:256], atw[:, 256:768],
                            atw[:, 768:1024], atw[:, 1024:1280])
                return (mz_sb[:, 0:256], scr[:, scr8:scr8 + 512],
                        scr[:, scr8 + 512:scr8 + 768],
                        scr[:, scr8 + 768:scr8 + 1024])

            def emit_stageA(hh, part):
                base = part * BLK
                cpo = (part % 2) * 512
                SB = Bp[0] if part % 2 == 0 else Dp
                ks, vs, qs = slot(hh, 1), slot(hh, 2), slot(hh, 0)
                wv = scaleVe.get(part % 2)

                @block.tensor
                def _(t, hh=hh, part=part, base=base, cpo=cpo, SB=SB,
                      ks=ks, vs=vs, qs=qs, wv=wv):
                    if part == 0:
                        t.wait_ge(SM["ac"], C["ac"])
                    if wv is None:
                        t.wait_ge(SM["ve"], C["ve"])
                    else:
                        t.wait_ge(SM["ve"], wv)
                    for c2 in range(2):
                        t.transpose(Cp[:, cpo + c2 * 128:cpo + (c2 + 1) * 128],
                                    ks[:, base + c2 * 128:base + (c2 + 1) * 128],
                                    smal[:, 0:128])
                    for c2 in range(2):
                        ins = t.transpose(
                            Cp[:, cpo + 256 + c2 * 128:cpo + 256 + (c2 + 1) * 128],
                            vs[:, base + c2 * 128:base + (c2 + 1) * 128],
                            smal[:, 0:128])
                    ins.then_inc(*inc("mm", 1))
                    for c2 in range(2):
                        ins = t.matmul(SB[:, c2 * 256:(c2 + 1) * 256],
                                       ks[:, base + c2 * 128:base + (c2 + 1) * 128],
                                       qs[:, base:base + 256],
                                       start=True, stop=True)
                    ins.then_inc(*inc("mm", 1))
                return C["mm"]

            def emit_scale(hh, part, mm_tv):
                tp2 = hh % 2
                jg = hf * 4 + part
                base = part * BLK
                cpo = (part % 2) * 512
                SB = Bp[0] if part % 2 == 0 else Dp
                kT, scb, qdb_r, vT = atw_reg(part)
                qs = slot(hh, 0)

                @block.vector
                def _(v, hh=hh, jg=jg, base=base, mm_tv=mm_tv, tp2=tp2,
                      cpo=cpo, SB=SB, kT=kT, scb=scb, qdb_r=qdb_r, vT=vT,
                      qs=qs):
                    v.wait_ge(SM["mm"], mm_tv)
                    for c2 in range(2):
                        v.tensor_single_scalar(
                            kT[:, c2 * 128:(c2 + 1) * 128],
                            Cp[:, cpo + c2 * 128:cpo + (c2 + 1) * 128],
                            colbuf[:, hh * 2 + c2:hh * 2 + c2 + 1], ALU.mult)
                        v.tensor_copy(
                            vT[:, c2 * 128:(c2 + 1) * 128],
                            Cp[:, cpo + 256 + c2 * 128:cpo + 256 + (c2 + 1) * 128])
                    for c2 in range(2):
                        v.tensor_mul(
                            scb[:, c2 * 256:(c2 + 1) * 256],
                            SB[:, c2 * 256:(c2 + 1) * 256],
                            diag_sb[:, tp2 * 512 + c2 * 256:
                                    tp2 * 512 + (c2 + 1) * 256])
                    if jg > 0:
                        v.tensor_mul(qdb_r[:, 0:256],
                                     qs[:, base:base + 256],
                                     qd_sb[:, tp2 * BLK:(tp2 + 1) * BLK])\
                            .then_inc(*inc("ve", 1))
                    else:
                        v.tensor_copy(qdb_r[:, 0:8], qdb_r[:, 0:8])\
                            .then_inc(*inc("ve", 1))
                scaleVe[part % 2] = C["ve"]
                return C["ve"]

            def emit_stageB(hh, part, ve_ds):
                jg = hf * 4 + part
                tp2 = hh % 2
                kvbr = kvb_sb[:, tp2 * 128:(tp2 + 1) * 128]
                kT, scb, qdb_r, vT = atw_reg(part)

                @block.tensor
                def _(t, jg=jg, ve_ds=ve_ds, kvbr=kvbr, kT=kT, scb=scb,
                      qdb_r=qdb_r, vT=vT):
                    t.wait_ge(SM["ve"], ve_ds)
                    nmm = 3 if jg > 0 else 2
                    kk = 0
                    for c2 in range(2):
                        kk += 1
                        ins = t.matmul(
                            Bp[1][:, 0:256],
                            vT[:, c2 * 128:(c2 + 1) * 128],
                            scb[:, c2 * 256:(c2 + 1) * 256],
                            start=(kk == 1), stop=(kk == nmm))
                    if jg > 0:
                        ins = t.matmul(Bp[1][:, 0:256], kvbr,
                                       qdb_r[:, 0:256],
                                       start=False, stop=True)
                    ins.then_inc(*inc("mm", 1))
                    for c2 in range(2):
                        ins = t.matmul(
                            Bp[1][:, 256:384],
                            kT[:, c2 * 128:(c2 + 1) * 128],
                            vT[:, c2 * 128:(c2 + 1) * 128],
                            start=(c2 == 0), stop=(c2 == 1))
                    ins.then_inc(*inc("mm", 1))
                return C["mm"]

            def emit_kv(hh, part, mm_kv):
                tp2 = hh % 2
                base = part * BLK
                kvr = kv_sb[:, tp2 * 128:(tp2 + 1) * 128]
                kvbr = kvb_sb[:, tp2 * 128:(tp2 + 1) * 128]
                apb = attnp_buf(hh)

                @block.vector
                def _(v, hh=hh, base=base, mm_kv=mm_kv, kvr=kvr, kvbr=kvbr,
                      apb=apb):
                    v.wait_ge(SM["mm"], mm_kv)
                    v.tensor_copy(apb[:, base:base + 256], Bp[1][:, 0:256])
                    vF(v, v.scalar_tensor_tensor(kvr, kvr, BD_TAB[hh],
                                                 Bp[1][:, 256:384],
                                                 ALU.mult, ALU.add))
                    v.tensor_copy(kvbr, kvr).then_inc(*inc("ve", 1))

            def emit_scan_end(hh):
                scanEnd[hh] = C["ve"]
                ve_head = C["ve"]
                tp2 = hh % 2
                kvr = kv_sb[:, tp2 * 128:(tp2 + 1) * 128]
                apb = attnp_buf(hh)

                @block.sync
                def _(s, hh=hh, ve_head=ve_head, apb=apb):
                    s.wait_ge(SM["ve"], ve_head)
                    s.dma_start(out=attnp_out[hh][:, hf * TH:(hf + 1) * TH],
                                in_=apb).then_inc(*inc("xs"))
                attnpSt[hh] = C["xs"]

                @block.gpsimd
                def _(g, hh=hh, ve_head=ve_head, kvr=kvr):
                    g.wait_ge(SM["ve"], ve_head)
                    g.dma_start(out=s_src[hh * 128:(hh + 1) * 128, :],
                                in_=kvr).then_inc(*inc("gs"))

            for qk in range(3):
                load_qkv_part(0, qk)
            load_tables(0)

            def scan_pair(sh, part):
                mm_tv = emit_stageA(sh, part)
                emit_scale(sh, part, mm_tv)

            def scan_fin(sh, part):
                ve_ds = scaleVe[part % 2]
                mm_kv = emit_stageB(sh, part, ve_ds)
                emit_kv(sh, part, mm_kv)

            for h in range(H1 + 1):
                sh = h - 1
                ch = h < H1
                if sh >= 0:
                    scan_pair(sh, 0)
                if ch:
                    emit_subchain(h, 0, 0, 16)
                if sh >= 0:
                    scan_fin(sh, 0)
                    scan_pair(sh, 1)
                if ch:
                    emit_subchain(h, 0, 16, 32)
                    if h + 1 < H1:
                        load_qkv_part(h + 1, 0)
                if sh >= 0:
                    scan_fin(sh, 1)
                    scan_pair(sh, 2)
                if ch:
                    emit_subchain(h, 1, 0, 16)
                if sh >= 0:
                    scan_fin(sh, 2)
                    scan_pair(sh, 3)
                if ch:
                    emit_subchain(h, 1, 16, 32)
                    if h + 1 < H1:
                        load_qkv_part(h + 1, 1)
                if sh >= 0:
                    scan_fin(sh, 3)
                if ch:
                    emit_subchain(h, 2, 0, 16)
                if sh >= 0:
                    emit_scan_end(sh)
                    if hf == 1 and sh in (15, 31):
                        gs_c = C["gs"]
                        csem = "c1" if sh == 15 else "c3"
                        ssl = s_src[0:16 * 128, :] if sh == 15 \
                            else s_src[16 * 128:32 * 128, :]
                        sout = sall1 if sh == 15 else sall2

                        @block.gpsimd
                        def _(g, gs_c=gs_c, csem=csem, ssl=ssl, sout=sout):
                            g.wait_ge(SM["gs"], gs_c)
                            g.collective_compute(
                                "AllGather", ALU.bypass,
                                replica_groups=[[0, 1, 2, 3], [4, 5, 6, 7]],
                                ins=[ssl.opt()],
                                outs=[sout.opt()]).then_inc(SM[csem], 1)
                if ch:
                    emit_subchain(h, 2, 16, 32)
                    if h + 1 < H1:
                        load_qkv_part(h + 1, 2)
                    emit_head_start(h)
                    if h + 1 < H1:
                        load_tables(h + 1)

            slabF = [None, None]
            gbankF = None
            hbankF = None
            gstgF = [None, None]
            xstrF = [None, None]
            houtF = [None, None]
            for o in range(32):
                p = o % 2
                wbase = p * 8320

                @block.gpsimd
                def _(g, o=o, p=p, wbase=wbase):
                    if slabF[p] is None:
                        g.wait_ge(SM["mm"], C["mm"])
                    else:
                        g.wait_ge(SM["mm"], slabF[p])
                    g.dma_start(
                        out=wslab[:, wbase:wbase + 8192]
                        .rearrange("p (c m) -> p c m", c=64),
                        in_=gw1_slabs[o].rearrange("(c p) m -> p c m", p=128))\
                        .then_inc(*inc("wl"))
                wl_o = C["wl"]

                @block.sync
                def _(s, o=o, p=p):
                    if xstrF[p] is None:
                        s.wait_ge(SM["ve"], C["ve"])
                        s.wait_ge(SM["xs"], C["xs"])
                    else:
                        s.wait_ge(SM["ve"], xstrF[p])
                    s.dma_start(out=f32buf[:, p * TH:(p + 1) * TH],
                                in_=x1_dram[o][:, hf * TH:(hf + 1) * TH])\
                        .then_inc(*inc("al"))
                al_o = C["al"]

                def evac_gt(tc, ps, mmv, o=o, p=p):
                    @block.scalar
                    def _(sc, tc=tc, mmv=mmv, p=p):
                        sc.wait_ge(SM["mm"], mmv)
                        if gstgF[p] is not None:
                            sc.wait_ge(SM["xs"], gstgF[p])
                        else:
                            sc.wait_ge(SM["xs"], C["xs"])
                            sc.wait_ge(SM["ve"], C["ve"])
                        sc.activation(scr[:, p * TH + tc * 512:
                                          p * TH + (tc + 1) * 512],
                                      ps[:, :], AF.Sigmoid).then_inc(*inc("ac", 1))

                def wait_gt(t, wl_o=wl_o):
                    nonlocal gbankF
                    t.wait_ge(SM["wl"], wl_o)
                    if gbankF is None:
                        t.wait_ge(SM["ac"], C["ac"])
                        t.wait_ge(SM["ve"], C["ve"])
                    else:
                        t.wait_ge(SM["ac"], gbankF)

                chain(32, lambda d, wbase=wbase: wslab[:, wbase + d * 128:
                                                       wbase + (d + 1) * 128],
                      lambda d, tc: actT[:, d * TH + tc * 512:
                                         d * TH + (tc + 1) * 512],
                      evac_gt, banks=(0, 1), wait_before=wait_gt)
                gbankF = C["ac"]
                ac_g = C["ac"]

                @block.sync
                def _(s, o=o, p=p, ac_g=ac_g):
                    s.wait_ge(SM["ac"], ac_g)
                    s.dma_start(out=gate_out[o][:, hf * TH:(hf + 1) * TH],
                                in_=scr[:, p * TH:(p + 1) * TH])\
                        .then_inc(*inc("xs"))
                gstgF[p] = C["xs"]

                def evac_h(tc, ps, mmv, o=o, p=p, al_o=al_o):
                    ob = 2048 + p * TH
                    @block.scalar
                    def _(sc, tc=tc, mmv=mmv, o=o, p=p, ob=ob):
                        sc.wait_ge(SM["mm"], mmv)
                        if houtF[p] is not None:
                            sc.wait_ge(SM["xs"], houtF[p])
                        else:
                            sc.wait_ge(SM["xs"], C["xs"])
                        sc.activation(
                            f32buf[:, ob + tc * 512:ob + (tc + 1) * 512],
                            ps[:, :], AF.Identity, bias=bias_sb[:, 40 + o:41 + o])\
                            .then_inc(*inc("ac", 1))
                    ac_h = C["ac"]

                    @block.vector
                    def _(v, tc=tc, ac_h=ac_h, o=o, p=p, al_o=al_o, ob=ob):
                        v.wait_ge(SM["ac"], ac_h)
                        v.wait_ge(SM["al"], al_o)
                        if o == 0:
                            v.tensor_copy(
                                hs_sb[0:8, hf * 512 + tc * 256:
                                      hf * 512 + (tc + 1) * 256],
                                f32buf[0:8, ob + tc * 512 + 1:
                                       ob + (tc + 1) * 512:2])
                        v.tensor_add(
                            f32buf[:, ob + tc * 512:ob + (tc + 1) * 512],
                            f32buf[:, ob + tc * 512:ob + (tc + 1) * 512],
                            f32buf[:, p * TH + tc * 512:p * TH + (tc + 1) * 512])\
                            .then_inc(*inc("ve", 1))

                def wait_h(t):
                    nonlocal hbankF
                    if hbankF is None:
                        t.wait_ge(SM["ac"], C["ac"])
                        t.wait_ge(SM["ve"], C["ve"])
                    else:
                        t.wait_ge(SM["ac"], hbankF[0])
                        t.wait_ge(SM["ve"], hbankF[1])

                chain(32, lambda d, wbase=wbase: wslab[:, wbase + 4096 + d * 128:
                                                       wbase + 4096 + (d + 1) * 128],
                      lambda d, tc: actT[:, d * TH + tc * 512:
                                         d * TH + (tc + 1) * 512],
                      evac_h, banks=(2, 3), wait_before=wait_h)
                slabF[p] = C["mm"]
                hbankF = (C["ac"], C["ve"])
                xstrF[p] = C["ve"]
                ve_h = C["ve"]

                @block.sync
                def _(s, o=o, p=p, ve_h=ve_h):
                    s.wait_ge(SM["ve"], ve_h)
                    s.dma_start(out=r_out[o][:, hf * TH:(hf + 1) * TH],
                                in_=f32buf[:, 2048 + p * TH:2048 + (p + 1) * TH])\
                        .then_inc(*inc("xs"))
                houtF[p] = C["xs"]

            @block.gpsimd
            def _(g):
                g.wait_ge(SM["mm"], C["mm"])
                g.dma_start(out=wslab[:, 0:1024]
                            .rearrange("p (c m) -> p c m", c=32),
                            in_=a_l1.rearrange("(c p) m -> p c m", p=128))\
                    .then_inc(*inc("wl"))
            wl_z = C["wl"]

            for tc in range(2):
                @block.tensor
                def _(t, tc=tc):
                    t.wait_ge(SM["wl"], wl_z)
                    t.wait_ge(SM["ve"], C["ve"])
                    for d in range(32):
                        ins = t.matmul(Bp[0][0:32, :],
                                       wslab[:, d * 32:(d + 1) * 32],
                                       actT[:, d * TH + tc * 512:
                                            d * TH + (tc + 1) * 512],
                                       start=(d == 0), stop=(d == 31))
                    ins.then_inc(*inc("mm", 1))
                mm_z = C["mm"]

                @block.vector
                def _(v, tc=tc, mm_z=mm_z):
                    v.wait_ge(SM["mm"], mm_z)
                    v.tensor_copy(zt_sb[0:32, hf * TH + tc * 512:
                                        hf * TH + (tc + 1) * 512],
                                  Bp[0][0:32, :]).then_inc(*inc("ve", 1))

        run_half_l1(0)
        run_half_l1(1)

        @block.vector
        def _(v):
            wall(v, "ve", "mm")
            v.tensor_copy(scr[0:32, 0:T], zt_sb[0:32, :])\
                .then_inc(*inc("ve", 1))
        ve_a = C["ve"]

        for tc in range(2):
            @block.tensor
            def _(t, tc=tc):
                t.wait_ge(SM["ve"], ve_a)
                t.wait_ge(SM["ve"], C["ve"])
                ins = t.matmul(Bp[1][0:8, :], smal[0:32, 257:265],
                               scr[0:32, 1 + 2 * tc * 512:
                                   1 + 2 * (tc + 1) * 512:2],
                               start=True, stop=True)
                ins.then_inc(*inc("mm", 1))
            mm_d = C["mm"]

            @block.vector
            def _(v, tc=tc, mm_d=mm_d):
                v.wait_ge(SM["mm"], mm_d)
                v.tensor_copy(hs_sb[0:8, 1024 + tc * 512:
                                    1024 + (tc + 1) * 512],
                              Bp[1][0:8, :]).then_inc(*inc("ve", 1))

        @block.vector
        def _(v):
            wall(v, "ve")
            v.reduce_max(colbuf[0:8, 84:85], hs_sb[0:8, 0:1024], AX,
                         apply_absolute_value=True)
            v.reduce_max(colbuf[0:8, 85:86], hs_sb[0:8, 1024:2048], AX,
                         apply_absolute_value=True).then_inc(*inc("ve", 1))
        ve_mx = C["ve"]

        @block.sync
        def _(s):
            s.wait_ge(SM["ve"], ve_mx)
            s.dma_start(out=mxb[0:8, :], in_=colbuf[0:8, 84:85])\
                .then_inc(*inc("xs"))
            s.dma_start(out=mxb[8:16, :], in_=colbuf[0:8, 85:86])\
                .then_inc(*inc("xs"))
        xs_mx = C["xs"]

        @block.sync
        def _(s):
            s.wait_ge(SM["xs"], xs_mx)
            s.dma_start(out=mrow[0:1, 0:16], in_=mxb.rearrange("a b -> b a"))\
                .then_inc(*inc("al"))
        al_mx = C["al"]

        @block.vector
        def _(v):
            v.wait_ge(SM["al"], al_mx)
            v.reduce_max(mrow[0:1, 16:17], mrow[0:1, 0:8], AX)
            v.reduce_max(mrow[0:1, 17:18], mrow[0:1, 8:16], AX)\
                .then_inc(*inc("ve", 1))
        ve_mr = C["ve"]

        @block.sync
        def _(s):
            s.wait_ge(SM["ve"], ve_mr)
            s.dma_start(out=mx_src.rearrange("a b -> b a"),
                        in_=mrow[0:1, 16:18]).then_inc(*inc("xs"))

        @block.gpsimd
        def _(g):
            g.wait_ge(SM["xs"], C["xs"])
            g.wait_ge(SM["gs"], C["gs"])
            g.collective_compute(
                "AllGather", ALU.bypass,
                replica_groups=[[0, 1, 2, 3, 4, 5, 6, 7]],
                ins=[mx_src.opt()], outs=[mxg.opt()]).then_inc(SM["c2"], 1)

        fence_at = dict(C)

        def fence(eng):
            for n in ["wl", "al", "xb", "mm", "ac", "ve", "gs", "xs"]:
                if fence_at[n] > 0:
                    eng.wait_ge(SM[n], fence_at[n])
            eng.wait_ge(SM["c1"], 1)
            eng.wait_ge(SM["c2"], 1)
            eng.wait_ge(SM["c3"], 1)

        @block.sync
        def _(s):
            fence(s)

        @block.gpsimd
        def _(g):
            fence(g)

        @block.vector
        def _(v):
            fence(v)

        @block.scalar
        def _(sc):
            fence(sc)

        @block.tensor
        def _(t):
            fence(t)

        gnT = actT
        kvin = scr
        qdb = scr
        f32a = f32buf
        R2 = bcast
        gatef = attnp_sb
        sprd = diag_sb

        @block.gpsimd
        def _(g):
            g.dma_start(out=wslab[0:32, 8192:12288], in_=bmat)\
                .then_inc(*inc("wl"))
            for o in range(2):
                g.dma_start(
                    out=wslab[:, o * 4096:(o + 1) * 4096]
                    .rearrange("p (c m) -> p c m", c=32),
                    in_=outw_slabs[o].rearrange("(c p) m -> p c m", p=128))\
                    .then_inc(*inc("wl"))
        pre_wl = {0: C["wl"] - 16, 1: C["wl"]}

        @block.sync
        def _(s):
            s.dma_start(out=mrow[0:1, 0:16], in_=mxg.rearrange("a b -> b a"))\
                .then_inc(*inc("al"))
        al_gx = C["al"]

        @block.vector
        def _(v):
            v.wait_ge(SM["al"], al_gx)
            vF(v, v.reduce_max(mrow[0:1, 18:19], mrow[0:1, 0:16:2], AX))
            vF(v, v.reduce_max(mrow[0:1, 19:20], mrow[0:1, 1:16:2], AX))
            vF(v, v.reciprocal(mrow[0:1, 20:22], mrow[0:1, 18:20]))
            v.tensor_scalar_mul(mrow[0:1, 20:22], mrow[0:1, 20:22], 0.5)\
                .then_inc(*inc("ve", 1))
        ve_gx = C["ve"]

        @block.tensor
        def _(t):
            t.wait_ge(SM["ve"], ve_gx)
            t.matmul(A[0][:, 0:1], onesr[0:1, :], mrow[0:1, 20:21],
                     start=True, stop=True)
            t.matmul(A[1][:, 0:1], onesr[0:1, :], mrow[0:1, 21:22],
                     start=True, stop=True).then_inc(*inc("mm", 1))
        mm_gx = C["mm"]

        @block.vector
        def _(v):
            v.wait_ge(SM["mm"], mm_gx)
            v.tensor_copy(colbuf[:, 0:1], A[0][:, 0:1])
            v.tensor_copy(colbuf[:, 1:2], A[1][:, 0:1]).then_inc(*inc("ve", 1))
        ve_cb = C["ve"]

        def emit_t2_it(it):
            ve_pv = C["ve"]

            @block.tensor
            def _(t, it=it, ve_pv=ve_pv):
                t.wait_ge(SM["ve"], ve_pv)
                t.transpose(Dp[:, 0:8],
                            hs_sb[0:8, it * 128:(it + 1) * 128],
                            idf[0:8, 0:8])
                t.transpose(Dp[:, 8:16],
                            hs_sb[0:8, 1024 + it * 128:1024 + (it + 1) * 128],
                            idf[0:8, 0:8]).then_inc(*inc("mm", 1))
            mm_tt = C["mm"]

            @block.vector
            def _(v, mm_tt=mm_tt):
                v.wait_ge(SM["mm"], mm_tt)
                vF(v, v.tensor_copy(ltile[:, 0:16], Dp[:, 0:16]))
                v.tensor_single_scalar(ltile[:, 16:24], ltile[:, 0:8],
                                       colbuf[:, 0:1], ALU.mult)
                vF(v, v.tensor_single_scalar(ltile[:, 24:32], ltile[:, 8:16],
                                             colbuf[:, 1:2], ALU.mult))
                vF(v, v.tensor_add(ltile[:, 32:40], ltile[:, 16:24],
                                   ltile[:, 24:32]))
                vF(v, v.reduce_max(colbuf[:, 2:3], ltile[:, 32:40], AX))
                v.tensor_scalar_mul(colbuf[:, 3:4], colbuf[:, 2:3], -1.0)\
                    .then_inc(*inc("ve", 1))
            ve_lg = C["ve"]

            @block.scalar
            def _(sc, ve_lg=ve_lg):
                sc.wait_ge(SM["ve"], ve_lg)
                sc.activation(ltile[:, 40:48], ltile[:, 32:40], AF.Exp,
                              bias=colbuf[:, 3:4]).then_inc(*inc("ac", 1))
            ac_e = C["ac"]

            @block.vector
            def _(v, ac_e=ac_e):
                v.wait_ge(SM["ac"], ac_e)
                vF(v, v.reduce_sum(colbuf[:, 4:5], ltile[:, 40:48], AX))
                vF(v, v.reciprocal(colbuf[:, 5:6], colbuf[:, 4:5]))
                vF(v, v.tensor_single_scalar(ltile[:, 48:56], ltile[:, 40:48],
                                             colbuf[:, 5:6], ALU.mult))
                vF(v, v.reduce_max(colbuf[:, 6:7], ltile[:, 48:56], AX))
                v.tensor_single_scalar(ltile[:, 56:64], ltile[:, 48:56],
                                       colbuf[:, 6:7], ALU.subtract)\
                    .then_inc(*inc("ve", 1))
            ve_p = C["ve"]

            @block.scalar
            def _(sc, ve_p=ve_p):
                sc.wait_ge(SM["ve"], ve_p)
                aF(sc, sc.activation(ltile[:, 64:72], ltile[:, 56:64],
                                     AF.Sign))
                sc.activation(ltile[:, 72:80], ltile[:, 64:72], AF.Identity,
                              bias=bias_sb[:, 82:83], scale=bias_sb[:, 82:83])\
                        .then_inc(*inc("ac", 1))
            ac_s = C["ac"]

            @block.vector
            def _(v, ac_s=ac_s):
                v.wait_ge(SM["ac"], ac_s)
                vF(v, v.tensor_sub(ltile[:, 80:88], ltile[:, 48:56],
                                   ltile[:, 72:80]))
                vF(v, v.reduce_max(colbuf[:, 7:8], ltile[:, 80:88], AX))
                v.tensor_single_scalar(ltile[:, 88:96], ltile[:, 48:56],
                                       colbuf[:, 7:8], ALU.subtract)\
                    .then_inc(*inc("ve", 1))
            ve_th = C["ve"]

            @block.scalar
            def _(sc, ve_th=ve_th):
                sc.wait_ge(SM["ve"], ve_th)
                sc.activation(ltile[:, 96:104], ltile[:, 88:96], AF.Sigmoid,
                              scale=2.0).then_inc(*inc("ac", 1))
            ac_m = C["ac"]

            @block.vector
            def _(v, ac_m=ac_m):
                v.wait_ge(SM["ac"], ac_m)
                vF(v, v.tensor_mul(ltile[:, 104:112], ltile[:, 48:56],
                                   ltile[:, 96:104]))
                vF(v, v.reduce_sum(colbuf[:, 8:9], ltile[:, 104:112], AX))
                vF(v, v.tensor_scalar_add(colbuf[:, 9:10], colbuf[:, 8:9],
                                          1e-9))
                vF(v, v.reciprocal(colbuf[:, 10:11], colbuf[:, 9:10]))
                vF(v, v.tensor_single_scalar(ltile[:, 112:120],
                                             ltile[:, 104:112],
                                             colbuf[:, 10:11], ALU.mult))
                v.tensor_copy(scr[:, 6656:6664], ltile[:, 112:120])\
                    .then_inc(*inc("ve", 1))
            ve_w = C["ve"]

            @block.tensor
            def _(t, ve_w=ve_w):
                t.wait_ge(SM["ve"], ve_w)
                t.transpose(Cp[0:8, 0:128], scr[:, 6656:6664],
                            smal[:, 0:128]).then_inc(*inc("mm", 1))
            mm_w = C["mm"]

            @block.vector
            def _(v, it=it, mm_w=mm_w):
                v.wait_ge(SM["mm"], mm_w)
                v.tensor_copy(qdbd_sb[0:8, it * 128:(it + 1) * 128],
                              Cp[0:8, 0:128]).then_inc(*inc("ve", 1))

        def emit_t2_tail():
            @block.vector
            def _(v):
                wall(v, "ve")
                v.tensor_copy(lgT[0:8, 0:T:2], qdbd_sb[0:8, 0:T // 2])
                v.tensor_copy(lgT[0:8, 1:T:2], qdbd_sb[0:8, 0:T // 2])\
                    .then_inc(*inc("ve", 1))
            ve_wf = C["ve"]

            for tc in range(4):
                @block.tensor
                def _(t, tc=tc, ve_wf=ve_wf):
                    t.wait_ge(SM["ve"], ve_wf)
                    if tc > 0:
                        t.wait_ge(SM["ve"], C["ve"])
                    t.matmul(Bp[0][0:32, :], limes_sb[0:8, :],
                             lgT[0:8, tc * 512:(tc + 1) * 512],
                             start=True, stop=True).then_inc(*inc("mm", 1))
                mm_p = C["mm"]

                @block.vector
                def _(v, tc=tc, mm_p=mm_p):
                    v.wait_ge(SM["mm"], mm_p)
                    v.scalar_tensor_tensor(scr[0:32, 7168 + tc * 512:
                                               7168 + (tc + 1) * 512],
                                           Bp[0][0:32, :], 0.5,
                                           zt_sb[0:32, tc * 512:(tc + 1) * 512],
                                           ALU.mult, ALU.mult)\
                        .then_inc(*inc("ve", 1))

        kvinVe = {}
        spF = [None, None]
        for h in range(H1):
            ph3 = h % 2
            sbase = ph3 * 512

            @block.gpsimd
            def _(g, h=h, ph3=ph3, sbase=sbase):
                if spF[ph3] is None:
                    g.wait_ge(SM["ve"], C["ve"])
                else:
                    g.wait_ge(SM["ve"], spF[ph3])
                sg = sall1 if h < 16 else sall2
                hh = h % 16
                for sp in range(4):
                    g.dma_start(
                        out=diag_sb[:, sbase + sp * 128:sbase + (sp + 1) * 128],
                        in_=sg[sp * 16 * 128 + hh * 128:
                               sp * 16 * 128 + (hh + 1) * 128, :])\
                        .then_inc(*inc("gs"))
            al_s = C["gs"]

            @block.vector
            def _(v, h=h, sbase=sbase, al_s=al_s):
                v.wait_ge(SM["gs"], al_s)
                kvt = bcast[:, 0:128]
                vF(v, v.tensor_single_scalar(
                    kvt, diag_sb[:, sbase:sbase + 128],
                    coefq[:, h:h + 1], ALU.mult))
                for sp in range(1, 4):
                    vF(v, v.scalar_tensor_tensor(
                        kvt, diag_sb[:, sbase + sp * 128:sbase + (sp + 1) * 128],
                        coefq[:, sp * H1 + h:sp * H1 + h + 1],
                        kvt, ALU.mult, ALU.add))
                v.tensor_copy(kvin[:, h * 128:(h + 1) * 128], kvt)\
                    .then_inc(*inc("ve", 1))
            spF[ph3] = C["ve"]
            kvinVe[h] = C["ve"]

        corrMM = {}
        corrGs = {}
        evacVe = {}
        for h in range(H1):
            ph = h % 2
            fb = ph * 2048
            qb = ph * 2048
            bkp = (A[0], A[1]) if ph == 0 else (A[2], A[3])

            @block.sync
            def _(s, h=h, fb=fb):
                if corrGs.get(h - 2) is not None:
                    s.wait_ge(SM["gs"], corrGs[h - 2])
                else:
                    s.wait_ge(SM["xs"], C["xs"])
                s.dma_start(out=f32buf[:, fb:fb + T], in_=attnp_out[h])\
                    .then_inc(*inc("al"))
            al_c = C["al"]

            @block.scalar
            def _(sc, h=h, qb=qb):
                if corrMM.get(h - 2) is not None:
                    sc.wait_ge(SM["mm"], corrMM[h - 2])
                sc.dma_start(out=actT[:, qb:qb + T], in_=qdbd_out[h])\
                    .then_inc(*inc("xb"))
            xb_c = C["xb"]

            for tc in range(4):
                wv = evacVe.get((ph, tc % 2))

                @block.tensor
                def _(t, h=h, tc=tc, al_c=al_c, xb_c=xb_c, wv=wv, qb=qb,
                      bkp=bkp):
                    if tc == 0:
                        t.wait_ge(SM["al"], al_c)
                        t.wait_ge(SM["xb"], xb_c)
                        t.wait_ge(SM["ve"], kvinVe[h])
                    if wv is None:
                        t.wait_ge(SM["ve"], C["ve"])
                    else:
                        t.wait_ge(SM["ve"], wv)
                    t.matmul(bkp[tc % 2][:, :], kvin[:, h * 128:(h + 1) * 128],
                             actT[:, qb + tc * 512:qb + (tc + 1) * 512],
                             start=True, stop=True).then_inc(*inc("mm", 1))
                mm_c = C["mm"]

                @block.vector
                def _(v, tc=tc, mm_c=mm_c, fb=fb, bkp=bkp):
                    v.wait_ge(SM["mm"], mm_c)
                    v.tensor_add(f32buf[:, fb + tc * 512:fb + (tc + 1) * 512],
                                 f32buf[:, fb + tc * 512:fb + (tc + 1) * 512],
                                 bkp[tc % 2][:, :])
                    v.tensor_mul(scr[:, 6144:6656],
                                 f32buf[:, fb + tc * 512:fb + (tc + 1) * 512],
                                 f32buf[:, fb + tc * 512:fb + (tc + 1) * 512])\
                        .then_inc(*inc("ve", 1))
                evacVe[(ph, tc % 2)] = C["ve"]
                ve_sq = C["ve"]

                @block.tensor
                def _(t, h=h, tc=tc, ve_sq=ve_sq):
                    t.wait_ge(SM["ve"], ve_sq)
                    t.matmul(Bp[1][0:4, :],
                             smal[:, 272 + tc * 4:272 + (tc + 1) * 4],
                             scr[:, 6144:6656],
                             start=(h == 0 and tc == 0),
                             stop=(h == 31 and tc == 3))\
                        .then_inc(*inc("mm", 1))
            ve_corr = C["ve"]

            @block.gpsimd
            def _(g, h=h, fb=fb, ve_corr=ve_corr):
                g.wait_ge(SM["ve"], ve_corr)
                g.dma_start(out=corr_dram[h], in_=f32buf[:, fb:fb + T])\
                    .then_inc(*inc("gs"))
            corrGs[h] = C["gs"]
            corrMM[h] = C["mm"]
            if h < 8:
                emit_t2_it(h)

        emit_t2_tail()
        mm_rms = C["mm"]

        @block.vector
        def _(v):
            v.wait_ge(SM["mm"], mm_rms)
            v.tensor_copy(ltile[0:4, 0:512], Bp[1][0:4, :])\
                .then_inc(*inc("ve", 1))
        ve_rf = C["ve"]

        @block.sync
        def _(s):
            s.wait_ge(SM["ve"], ve_rf)
            s.dma_start(out=rms_dram, in_=ltile[0:4, 0:512])\
                .then_inc(*inc("xs"))
        xs_rf = C["xs"]

        @block.sync
        def _(s):
            s.wait_ge(SM["xs"], xs_rf)
            s.dma_start(out=rowbuf[0:1, 0:T],
                        in_=rms_dram.rearrange("(x a) b -> x (a b)", x=1))\
                .then_inc(*inc("al"))
        al_rf = C["al"]

        @block.vector
        def _(v):
            v.wait_ge(SM["al"], al_rf)
            v.tensor_scalar_mul(rowbuf[0:1, T:2 * T], rowbuf[0:1, 0:T],
                                1.0 / D).then_inc(*inc("ve", 1))
        ve_ms = C["ve"]

        @block.scalar
        def _(sc):
            sc.wait_ge(SM["ve"], ve_ms)
            sc.activation(rowbuf[0:1, 2 * T:3 * T], rowbuf[0:1, T:2 * T],
                          AF.Sqrt, bias=bias_sb[0:1, 81:82])\
                .then_inc(*inc("ac", 1))
        ac_r = C["ac"]

        @block.vector
        def _(v):
            v.wait_ge(SM["ac"], ac_r)
            v.reciprocal(rowbuf[0:1, T:2 * T], rowbuf[0:1, 2 * T:3 * T])\
                .then_inc(*inc("ve", 1))
        ve_rr = C["ve"]

        for tc in range(4):
            @block.tensor
            def _(t, tc=tc):
                t.wait_ge(SM["ve"], ve_rr)
                if tc > 0:
                    t.wait_ge(SM["ve"], C["ve"])
                t.matmul(A[0][:, :], onesr[0:1, :],
                         rowbuf[0:1, T + tc * 512:T + (tc + 1) * 512],
                         start=True, stop=True).then_inc(*inc("mm", 1))
            mm_r2 = C["mm"]

            @block.vector
            def _(v, tc=tc, mm_r2=mm_r2):
                v.wait_ge(SM["mm"], mm_r2)
                v.tensor_copy(R2[:, tc * 512:(tc + 1) * 512], A[0][:, :])\
                    .then_inc(*inc("ve", 1))

        def run_out(hf):
            gnldF = [None, None]
            gtldF = [None, None]
            gnVe = {}
            for h in range(H1):
                p = h % 2

                if p == 0:
                    @block.sync
                    def _(s, h=h, p=p):
                        if gnldF[p] is None:
                            s.wait_ge(SM["ve"], C["ve"])
                            s.wait_ge(SM["xs"], C["xs"])
                            s.wait_ge(SM["gs"], C["gs"])
                        else:
                            s.wait_ge(SM["ve"], gnldF[p])
                        s.dma_start(out=f32a[:, p * TH:(p + 1) * TH],
                                    in_=corr_dram[h][:, hf * TH:
                                                     (hf + 1) * TH])\
                            .then_inc(*inc("al"))
                    al_gn = ("al", C["al"])
                else:
                    @block.scalar
                    def _(sc, h=h, p=p):
                        if gnldF[p] is None:
                            sc.wait_ge(SM["ve"], C["ve"])
                            sc.wait_ge(SM["xs"], C["xs"])
                            sc.wait_ge(SM["gs"], C["gs"])
                        else:
                            sc.wait_ge(SM["ve"], gnldF[p])
                        sc.dma_start(out=f32a[:, p * TH:(p + 1) * TH],
                                     in_=corr_dram[h][:, hf * TH:
                                                      (hf + 1) * TH])\
                            .then_inc(*inc("xb"))
                    al_gn = ("xb", C["xb"])

                @block.gpsimd
                def _(g, h=h, p=p):
                    if gtldF[p] is None:
                        g.wait_ge(SM["ve"], C["ve"])
                    else:
                        g.wait_ge(SM["ve"], gtldF[p])
                    g.dma_start(out=gatef[:, 0:TH] if p == 0
                                else f32buf[:, 3072:4096],
                                in_=gate_out[h][:, hf * TH:(hf + 1) * TH])\
                        .then_inc(*inc("wl"))
                wl_gn = C["wl"]

                @block.vector
                def _(v, h=h, p=p, al_gn=al_gn, wl_gn=wl_gn):
                    v.wait_ge(SM[al_gn[0]], al_gn[1])
                    v.wait_ge(SM["wl"], wl_gn)
                    gbuf = gatef[:, 0:TH] if p == 0 else f32buf[:, 3072:4096]
                    v.tensor_single_scalar(f32a[:, p * TH:(p + 1) * TH],
                                           f32a[:, p * TH:(p + 1) * TH],
                                           nw_sb[:, h:h + 1], ALU.mult)
                    v.tensor_mul(f32a[:, p * TH:(p + 1) * TH],
                                 f32a[:, p * TH:(p + 1) * TH],
                                 R2[:, hf * TH:(hf + 1) * TH])
                    v.tensor_mul(gnT[:, h * TH:(h + 1) * TH],
                                 f32a[:, p * TH:(p + 1) * TH], gbuf)\
                        .then_inc(*inc("ve", 1))
                gnldF[p] = C["ve"]
                gtldF[p] = C["ve"]
                gnVe[h] = C["ve"]

            slabF = [None, None]
            bankF = [None, None]
            rldF = [None, None]
            ostgF = [None, None]
            for o in range(32):
                p = o % 2
                wb = p * 4096

                if hf == 0 and o < 2:
                    wl_o = pre_wl[o]
                else:
                    @block.gpsimd
                    def _(g, o=o, p=p, wb=wb):
                        if slabF[p] is None:
                            g.wait_ge(SM["mm"], C["mm"])
                        else:
                            g.wait_ge(SM["mm"], slabF[p])
                        g.dma_start(
                            out=wslab[:, wb:wb + 4096]
                            .rearrange("p (c m) -> p c m", c=32),
                            in_=outw_slabs[o]
                            .rearrange("(c p) m -> p c m", p=128))\
                            .then_inc(*inc("wl"))
                    wl_o = C["wl"]

                @block.scalar
                def _(sc, o=o, p=p):
                    if rldF[p] is None:
                        sc.wait_ge(SM["ve"], C["ve"])
                    else:
                        sc.wait_ge(SM["ve"], rldF[p])
                    sc.dma_start(out=f32buf[:, 2048 + p * TH:
                                            2048 + (p + 1) * TH],
                                 in_=r_out[o][:, hf * TH:(hf + 1) * TH])\
                        .then_inc(*inc("xb"))
                al_r = C["xb"]

                @block.tensor
                def _(t, o=o, p=p, wb=wb, wl_o=wl_o):
                    t.wait_ge(SM["wl"], wl_o)
                    if bankF[p] is not None:
                        t.wait_ge(SM["ve"], bankF[p])
                    bk = (A[0], A[1]) if p == 0 else (A[2], A[3])
                    for d in range(32):
                        if o < 6:
                            t.wait_ge(SM["ve"], gnVe[d])
                        for tc in range(2):
                            t.matmul(bk[tc][:, :],
                                     wslab[:, wb + d * 128:wb + (d + 1) * 128],
                                     gnT[:, d * TH + tc * 512:
                                         d * TH + (tc + 1) * 512],
                                     start=(d == 0), stop=False)
                    for tc in range(2):
                        t.matmul(bk[tc][:, :],
                                 wslab[0:32, 8192 + o * 128:8192 + (o + 1) * 128],
                                 scr[0:32, 7168 + hf * TH + tc * 512:
                                     7168 + hf * TH + (tc + 1) * 512],
                                 start=False, stop=True).then_inc(*inc("mm", 1))
                mm_o = C["mm"]
                slabF[p] = mm_o

                @block.vector
                def _(v, o=o, p=p, mm_o=mm_o, al_r=al_r):
                    v.wait_ge(SM["mm"], mm_o)
                    v.wait_ge(SM["xb"], al_r)
                    if ostgF[p] is not None:
                        v.wait_ge(SM["xs"], ostgF[p])
                    bk = (A[0], A[1]) if p == 0 else (A[2], A[3])
                    v.tensor_add(f32a[:, p * TH:p * TH + 512],
                                 f32buf[:, 2048 + p * TH:2048 + p * TH + 512],
                                 bk[0][:, :])
                    v.tensor_add(f32a[:, p * TH + 512:p * TH + 1024],
                                 f32buf[:, 2048 + p * TH + 512:
                                        2048 + p * TH + 1024],
                                 bk[1][:, :]).then_inc(*inc("ve", 1))
                ve_o = C["ve"]
                bankF[p] = ve_o
                rldF[p] = ve_o

                @block.sync
                def _(s, o=o, p=p, ve_o=ve_o):
                    s.wait_ge(SM["ve"], ve_o)
                    s.dma_start(out=out2[o * 128:(o + 1) * 128,
                                         hf * TH:(hf + 1) * TH],
                                in_=f32a[:, p * TH:(p + 1) * TH])\
                        .then_inc(*inc("xs"))
                ostgF[p] = C["xs"]

        run_out(0)
        run_out(1)

        @block.sync
        def _(s):
            s.wait_ge(SM["xs"], C["xs"])

        @block.gpsimd
        def _(g):
            g.wait_ge(SM["xs"], C["xs"])

    return nc



_NC = None


def kernel(**inputs):
    global _NC
    h = host_prep(inputs)
    if _NC is None:
        _NC = build_fused()
    nc = _NC

    shared = {k: h[k] for k in
              ["wq_slabs", "bq_sel", "Mt", "z8", "l0out_slabs", "b_base",
               "a_allT", "wgT", "qkv_slabs", "gw1_slabs",
               "b_l1", "a_l1", "limesT", "diagT", "qd_tab", "qdbd_tab",
               "kdec", "ones_bf", "ones_f32r", "ident_bf", "ident_f", "e16",
               "cst", "limes", "bmat", "norm_w", "outw_slabs", "sel4"]}
    in_maps = [dict(shared, x_tm=h["x_tm"][c], x_fm=h["x_fm"][c],
                    coefb4=h["coefb"][c])
               for c in range(NCORES)]
    res = run_bass_kernel_spmd(nc, in_maps, core_ids=list(range(NCORES)))
    outs = [np.ascontiguousarray(res.results[c]["out2"].T)
            for c in range(NCORES)]
    return np.concatenate(outs, axis=0).reshape(2, 8192, D)
